# version 01
# baseline (speedup 1.0000x reference)
import numpy as np
import ml_dtypes
import concourse.bass as bass
import concourse.mybir as mybir
from concourse.bass_utils import run_bass_kernel_spmd

F32 = mybir.dt.float32
BF16 = mybir.dt.bfloat16
AF = mybir.ActivationFunctionType
ALU = mybir.AluOpType
AX = mybir.AxisListType.X

NCORES = 8
B = 2
SEQ = 8192
D = 1024
T = 2048
HALO = 513
NEG = -1.0e30
BF = ml_dtypes.bfloat16

ENGINES = ("sync", "gpsimd", "tensor", "vector", "scalar")



def _host_consts():
    c = {}
    c["ones128"] = np.ones((128, 1), np.float32)
    c["ones1x128"] = np.ones((1, 128), np.float32)
    h2 = np.zeros((128, 2), np.float32)
    h2[0:64, 0] = 1.0
    h2[64:128, 1] = 1.0
    c["head2"] = h2
    h2e = np.zeros((2, 128), np.float32)
    h2e[0, 0:64] = 1.0
    h2e[1, 64:128] = 1.0
    c["head2exp"] = h2e
    c["ident"] = np.eye(128, dtype=np.float32)
    c["onesrow"] = np.ones((1, SEQ), np.float32)
    kp = np.zeros((4, SEQ), np.float32)
    kp[0, :] = 1.0
    c["kpre"] = kp
    qp = np.zeros((3, SEQ), np.float32)
    qp[0:2, :] = 1.0
    c["qpre"] = qp
    c["jrow"] = np.arange(SEQ, dtype=np.float32)[None, :]
    c["j16"] = np.tile(np.arange(SEQ, dtype=np.float32)[None, :], (16, 1))
    ht_ = np.arange(16) % 4
    c["ebits"] = (ht_ & 1).astype(np.float32)[:, None]

    mL = np.zeros((8, 128, 512), np.float32)
    for m in range(8):
        r = np.arange(128)[:, None]
        cc = np.arange(512)[None, :]
        diff = cc + 512 - 128 * m - r
        mL[m] = np.where((diff >= 0) & (diff <= 512), 0.0, NEG)
    c["maskL"] = np.ascontiguousarray(mL.transpose(1, 0, 2).reshape(128, 8 * 512))
    c["maskL0"] = np.ascontiguousarray(
        np.full((128, 4 * 512), NEG, np.float32))

    mC = np.zeros((5, 128, 512), np.float32)
    for rel in range(4):
        r = np.arange(128)[:, None]
        cc = np.arange(512)[None, :]
        mC[rel] = np.where(rel * 128 + r <= cc, 0.0, NEG)
    mC[4] = NEG
    c["maskC"] = np.ascontiguousarray(mC.transpose(1, 0, 2).reshape(128, 5 * 512))

    pos = np.arange(SEQ, dtype=np.float64)
    inv = 10000.0 ** (-(np.arange(0, 64, 2, dtype=np.float64) / 64))
    th = pos[None, :] * inv[:, None]
    cosA = np.cos(th).astype(np.float32)
    sinA = np.sin(th).astype(np.float32)
    c["att_t1"] = np.concatenate([cosA, cosA], 0)
    c["att_t2"] = np.concatenate([-sinA, sinA], 0)
    c["att_t1q"] = c["att_t1"] * 0.125
    c["att_t2q"] = c["att_t2"] * 0.125

    DH = 64
    WIN = 512
    half = np.arange(0, DH, 2, dtype=np.float64)
    invf = 1.0 / 10000.0 ** (half / DH)
    base = (half + 0.4 * DH) / (1.4 * DH)

    def tabs(tglob, expo):
        fr = tglob[None, :] * invf[:, None]
        sc = base[:, None] ** (expo[None, :] / 256.0)
        cosF = np.cos(fr) * sc
        sinF = np.sin(fr) * sc
        t1 = np.concatenate([cosF, cosF], 0)
        t2 = np.concatenate([-sinF, sinF], 0)
        return t1.astype(np.float32), t2.astype(np.float32)

    tmod = np.arange(SEQ, dtype=np.float64) % WIN
    tg = np.arange(SEQ, dtype=np.float64)
    c["ffq_t1"], c["ffq_t2"] = tabs(tg, tmod)
    c["ffka_t1"], c["ffka_t2"] = tabs(tg, -tmod)
    c["ffkb_t1"], c["ffkb_t2"] = tabs(tg, (WIN - tmod))
    c["ffq_t1"] = c["ffq_t1"] * 8.0
    c["ffq_t2"] = c["ffq_t2"] * 8.0
    return c


_CONSTS = None


def consts():
    global _CONSTS
    if _CONSTS is None:
        _CONSTS = _host_consts()
    return _CONSTS


def core_pos(c):
    return c // 4, (c % 4) * T


def _pp(vec, nchunk=8):
    v = np.asarray(vec, np.float32).reshape(nchunk, 128)
    return np.ascontiguousarray(v.T)


def _pp_pad(vec, nchunk, size):
    v = np.zeros(nchunk * 128, np.float32)
    v[:size] = np.asarray(vec, np.float32).ravel()[:size]
    return _pp(v, nchunk)


def _fm(x_btd):
    return np.ascontiguousarray(np.transpose(np.asarray(x_btd, np.float32), (0, 2, 1)))


def _slice_own(x_fm, c):
    b, t0 = core_pos(c)
    return np.ascontiguousarray(x_fm[b, :, t0:t0 + T])


def _slice_halo(x_fm, c, n=HALO):
    b, t0 = core_pos(c)
    out = np.zeros((D, n), np.float32)
    lo = t0 - n
    if lo >= 0:
        out[:] = x_fm[b, :, lo:t0]
    elif t0 > 0:
        out[:, n - t0:] = x_fm[b, :, 0:t0]
    return out


def _assemble_fm(outs):
    full = np.empty((B, D, SEQ), np.float32)
    for c in range(NCORES):
        b, t0 = core_pos(c)
        full[b, :, t0:t0 + T] = outs[c]
    return full



class Rec:
    def __init__(self, nc):
        self.nc = nc
        self.ops = {e: [] for e in ENGINES}
        self.s = {}
        self.c = {}

    def sem(self, n):
        if n not in self.s:
            self.s[n] = self.nc.alloc_semaphore(name="sm_" + n)
            self.c[n] = 0
        return self.s[n]

    def snap(self):
        return dict(self.c)

    def on(self, engine, fn, inc=None, inc_by=1, waits=None):
        if waits is None:
            waits = self.snap()
        waits = {n: v for n, v in waits.items() if v > 0}
        for n in waits:
            self.sem(n)
        if inc is not None:
            self.sem(inc)
            self.c[inc] += inc_by
        self.ops[engine].append((fn, waits, inc, inc_by))

    def dma(self, queue, out, in_, inc=None, slow=False):
        if inc is None:
            inc = {"sync": "dma", "gpsimd": "wdma", "scalar": "adma"}[queue]
        self.on(queue, lambda e, o=out, i=in_: e.dma_start(
            out=o, in_=i, allow_slow_non_contiguous=slow), inc=inc, inc_by=16)

    def finalize(self, block):
        for eng_name in ENGINES:
            ops = self.ops[eng_name]
            if not ops:
                continue

            def body(eng, ops=ops):
                last_waits = {}
                for fn, waits, inc, inc_by in ops:
                    for n, v in waits.items():
                        if last_waits.get(n, -1) < v:
                            eng.wait_ge(self.s[n], v)
                            last_waits[n] = v
                    li = fn(eng)
                    if inc is not None:
                        assert li is not None, f"op for inc={inc} returned None"
                        li.then_inc(self.s[inc], inc_by)
            getattr(block, eng_name)(body)


def _new_nc():
    return bass.Bass("TRN2", target_bir_lowering=False, debug=False)


def _din(nc, name, shape, dtype=F32):
    return nc.dram_tensor(name, list(shape), dtype, kind="ExternalInput").ap()


def _dout(nc, name, shape, dtype=F32):
    return nc.dram_tensor(name, list(shape), dtype, kind="ExternalOutput").ap()


def _sb(nc, name, shape, dtype=F32):
    return nc.alloc_sbuf_tensor(name, list(shape), dtype)


def _psum(nc, name, shape, dtype=F32):
    return nc.alloc_psum_tensor(name, list(shape), dtype)


def load_consts(r, nc, names):
    cst = consts()
    tiles = {}
    aps = {}
    for n in names:
        arr = cst[n]
        aps[n] = _din(nc, "c_" + n, arr.shape, F32)
        tiles[n] = _sb(nc, "t_" + n, arr.shape, F32)
        r.dma("sync", tiles[n][:], aps[n][:])
    return tiles, aps


def load_consts_bf(r, nc, names):
    cst = consts()
    tiles = {}
    aps = {}
    for n in names:
        arr = cst[n]
        aps[n] = _din(nc, "c_" + n, arr.shape, F32)
        tiles[n] = _sb(nc, "b_" + n, arr.shape, BF16)
        r.dma("gpsimd", tiles[n][:], aps[n][:])
    return tiles, aps


def const_in_map(names):
    cst = consts()
    return {"c_" + n: cst[n] for n in names}



class LNKit:
    def __init__(self, nc):
        self.sq = _sb(nc, "ln_sq", [128, 3 * 512], F32)
        self.st = _sb(nc, "ln_st", [1, 4 * 512], F32)
        self.nsc = _sb(nc, "ln_nsc", [2, 2 * 512], F32)
        self.ps_s0 = _psum(nc, "ln_ps0", [2, 512], F32)
        self.ps_s1 = _psum(nc, "ln_ps1", [1, 512], F32)
        self.ps_b0 = _psum(nc, "ln_pb0", [128, 512], F32)
        self.ps_b1 = _psum(nc, "ln_pb1", [128, 512], F32)


def rec_ln(r, kit, tiles, sl_in, sl_out, width, gb=None):
    nt = (width + 511) // 512
    for it in range(nt):
        j0 = it * 512
        w = min(512, width - j0)

        def sums_fn(t, j0=j0, w=w):
            for c in range(8):
                li = t.matmul(kit.ps_s0[0:1, 0:w], tiles["ones128"][:],
                              sl_in(c, j0, w), start=(c == 0), stop=(c == 7))
            return li
        r.on("tensor", sums_fn, inc="pe")
        for c in range(8):
            r.on("scalar", lambda s, c=c, j0=j0, w=w: s.activation(
                kit.sq[:, 0:w], sl_in(c, j0, w), AF.Square), inc="act")
            r.on("tensor", lambda t, c=c, w=w: t.matmul(
                kit.ps_s1[0:1, 0:w], tiles["ones128"][:], kit.sq[:, 0:w],
                start=(c == 0), stop=(c == 7)), inc="pe")

        MU, WRK, RSTD, NMR = (kit.st[0:1, i * 512:i * 512 + 512] for i in range(4))

        def st1_fn(v, w=w):
            v.tensor_scalar_mul(MU[:, 0:w], kit.ps_s0[0:1, 0:w], 1.0 / D)
            v.tensor_scalar_mul(WRK[:, 0:w], kit.ps_s1[0:1, 0:w], 1.0 / D)
            v.tensor_tensor(RSTD[:, 0:w], MU[:, 0:w], MU[:, 0:w], ALU.mult)
            v.tensor_tensor(WRK[:, 0:w], WRK[:, 0:w], RSTD[:, 0:w], ALU.subtract)
            return v.tensor_scalar_add(WRK[:, 0:w], WRK[:, 0:w], 1e-5)
        r.on("vector", st1_fn, inc="dve")

        r.on("scalar", lambda s, w=w: s.activation(NMR[:, 0:w], WRK[:, 0:w],
                                                   AF.Sqrt), inc="act")

        def st2_fn(v, w=w):
            v.reciprocal(RSTD[:, 0:w], NMR[:, 0:w])
            return v.scalar_tensor_tensor(NMR[:, 0:w], MU[:, 0:w], -1.0,
                                          RSTD[:, 0:w], ALU.mult, ALU.mult)
        r.on("vector", st2_fn, inc="dve")

        def bc_fn(t, w=w):
            t.matmul(kit.ps_b0[:, 0:w], tiles["ones1x128"][:], RSTD[:, 0:w],
                     start=True, stop=True)
            return t.matmul(kit.ps_b1[:, 0:w], tiles["ones1x128"][:], NMR[:, 0:w],
                            start=True, stop=True)
        r.on("tensor", bc_fn, inc="pe")

        def ap_fn(v, j0=j0, w=w):
            for c in range(8):
                src = sl_in(c, j0, w)
                dst = sl_out(c, j0, w)
                v.tensor_tensor(kit.sq[:, 0:w], src, kit.ps_b0[:, 0:w], ALU.mult)
                if gb is None:
                    li = v.tensor_tensor(dst, kit.sq[:, 0:w],
                                         kit.ps_b1[:, 0:w], ALU.add)
                else:
                    v.tensor_tensor(kit.sq[:, 0:w], kit.sq[:, 0:w],
                                    kit.ps_b1[:, 0:w], ALU.add)
                    li = v.tensor_scalar(dst, kit.sq[:, 0:w],
                                         gb[0][:, c:c + 1], gb[1][:, c:c + 1],
                                         ALU.mult, ALU.add)
            return li
        r.on("vector", ap_fn, inc="dve")


def rec_ln_stream(r, kit, tiles, stream, chnk, width, out=None, out_chnk=None, gb=None):
    def sl_in(c, j0, w):
        return stream[:, c * chnk + j0:c * chnk + j0 + w]
    if out is None:
        sl_out = sl_in
    else:
        def sl_out(c, j0, w):
            return out[:, c * out_chnk + j0:c * out_chnk + j0 + w]
    rec_ln(r, kit, tiles, sl_in, sl_out, width, gb=gb)



def rec_norm_rope(r, tiles, kit, ps_in, ps_n, ps_bc, scale_pp, t1, t2, tj0, w, dsts):
    sc1 = kit.sq[:, 0:512]
    sc2 = kit.sq[:, 512:1024]
    sc3 = kit.sq[:, 1024:1536]
    nrm = kit.nsc[0:2, 0:512]
    nri = kit.nsc[0:2, 512:1024]

    r.on("scalar", lambda s: s.activation(sc1[:, 0:w], ps_in[:, 0:w], AF.Square),
         inc="act")
    r.on("tensor", lambda t: t.matmul(ps_n[0:2, 0:w], tiles["head2"][:], sc1[:, 0:w],
                                      start=True, stop=True), inc="pe")
    r.on("scalar", lambda s: s.activation(nrm[:, 0:w], ps_n[0:2, 0:w], AF.Sqrt),
         inc="act")

    def clip_fn(v):
        v.tensor_scalar_max(nrm[:, 0:w], nrm[:, 0:w], 1e-12)
        return v.reciprocal(nri[:, 0:w], nrm[:, 0:w])
    r.on("vector", clip_fn, inc="dve")

    r.on("tensor", lambda t: t.matmul(ps_bc[:, 0:w], tiles["head2exp"][:],
                                      nri[:, 0:w], start=True, stop=True), inc="pe")
    r.on("scalar", lambda s: s.copy(sc1[:, 0:w], ps_in[:, 0:w]), inc="act")

    def fin_fn(v):
        v.scalar_tensor_tensor(sc2[:, 0:w], sc1[:, 0:w], scale_pp[:, 0:1],
                               ps_bc[:, 0:w], ALU.mult, ALU.mult)
        li = None
        for sub in range(2):
            b0 = 64 * sub
            dst = dsts[sub]
            v.tensor_scalar_mul(sc1[0:64, 0:w], sc2[b0:b0 + 64, 0:w], 1.0)
            v.tensor_scalar_mul(sc3[0:32, 0:w], sc2[b0 + 32:b0 + 64, 0:w], 1.0)
            v.tensor_scalar_mul(sc3[32:64, 0:w], sc2[b0:b0 + 32, 0:w], 1.0)
            v.tensor_tensor(sc1[0:64, 0:w], sc1[0:64, 0:w],
                            t1[0:64, tj0:tj0 + w], ALU.mult)
            v.tensor_tensor(sc3[0:64, 0:w], sc3[0:64, 0:w],
                            t2[0:64, tj0:tj0 + w], ALU.mult)
            li = v.tensor_tensor(dst[0:64, 0:w], sc1[0:64, 0:w], sc3[0:64, 0:w], ALU.add)
        return li
    r.on("vector", fin_fn, inc="dve")


def rec_rope_plain(r, kit, ps_in, t1, t2, tj0, w, dsts, do_rope, scl):
    sc1 = kit.sq[:, 0:512]
    sc2 = kit.sq[:, 512:1024]
    sc3 = kit.sq[:, 1024:1536]
    if not do_rope:
        def cp_fn(s):
            s.activation(dsts[0], ps_in[0:64, 0:w], AF.Copy, scale=scl)
            return s.activation(dsts[1], ps_in[64:128, 0:w], AF.Copy, scale=scl)
        r.on("scalar", cp_fn, inc="act")
        return
    r.on("scalar", lambda s: s.copy(sc1[:, 0:w], ps_in[:, 0:w]), inc="act")

    def fin_fn(v):
        li = None
        for sub in range(2):
            b0 = 64 * sub
            dst = dsts[sub]
            v.tensor_scalar_mul(sc2[0:64, 0:w], sc1[b0:b0 + 64, 0:w], 1.0)
            v.tensor_scalar_mul(sc3[0:32, 0:w], sc1[b0 + 32:b0 + 64, 0:w], 1.0)
            v.tensor_scalar_mul(sc3[32:64, 0:w], sc1[b0:b0 + 32, 0:w], 1.0)
            v.tensor_tensor(sc2[0:64, 0:w], sc2[0:64, 0:w],
                            t1[0:64, tj0:tj0 + w], ALU.mult)
            v.tensor_tensor(sc3[0:64, 0:w], sc3[0:64, 0:w],
                            t2[0:64, tj0:tj0 + w], ALU.mult)
            li = v.tensor_tensor(dst[0:64, 0:w], sc2[0:64, 0:w], sc3[0:64, 0:w], ALU.add)
        return li
    r.on("vector", fin_fn, inc="dve")



def rec_attn_blocks(r, kt, kdim, qt, vt, pt, ps_S, ps_O, qj0, qw, kblocks,
                    add_masks, vt_stride=65):
    n = len(kblocks)
    for i, kb in enumerate(kblocks):
        r.on("tensor", lambda t, kb=kb: t.matmul(
            ps_S[:, 0:qw], kt[0:kdim, kb * 128:(kb + 1) * 128],
            qt[0:kdim, qj0:qj0 + qw], start=True, stop=True), inc="pe")
        if add_masks is not None and add_masks[i] is not None:
            r.on("vector", lambda v, m=add_masks[i]: v.tensor_tensor(
                ps_S[:, 0:qw], ps_S[:, 0:qw], m, ALU.add), inc="dve")
        r.on("scalar", lambda s: s.activation(pt[:, 0:qw], ps_S[:, 0:qw], AF.Exp),
             inc="act")
        r.on("tensor", lambda t, kb=kb, i=i: t.matmul(
            ps_O[:, 0:qw], vt[:, kb * vt_stride:kb * vt_stride + 65],
            pt[:, 0:qw], start=(i == 0), stop=(i == n - 1)), inc="pe")


def rec_o_norm(r, tiles, kit, ps_O, ps_bc, osb, rcp, dst, w):
    r.on("vector", lambda v: v.reciprocal(rcp[0:1, 0:w], ps_O[64:65, 0:w]), inc="dve")
    r.on("scalar", lambda s: s.copy(osb[0:64, 0:w], ps_O[0:64, 0:w]), inc="act")
    r.on("tensor", lambda t: t.matmul(ps_bc[0:64, 0:w], tiles["ones1x128"][0:1, 0:64],
                                      rcp[0:1, 0:w], start=True, stop=True), inc="pe")

    def div_fn(v):
        v.tensor_tensor(kit.sq[0:64, 0:w], osb[0:64, 0:w], ps_bc[0:64, 0:w], ALU.mult)
        return v.tensor_scalar_mul(dst, kit.sq[0:64, 0:w], 1.0)
    r.on("vector", div_fn, inc="dve")



def rec_attn_win(r, kt, kdim, qt, vt, pt, ps_S, ps_O, qw, blocks):
    n = len(blocks)
    for i, (kbk, kbv, msk) in enumerate(blocks):
        r.on("tensor", lambda t, kbk=kbk: t.matmul(
            ps_S[:, 0:qw], kt[0:kdim, kbk * 128:(kbk + 1) * 128],
            qt[0:kdim, 0:qw], start=True, stop=True), inc="pe")
        if msk is not None:
            r.on("vector", lambda v, m=msk: v.tensor_tensor(
                ps_S[:, 0:qw], ps_S[:, 0:qw], m, ALU.add), inc="dve")
        r.on("scalar", lambda s: s.activation(pt[:, 0:qw], ps_S[:, 0:qw], AF.Exp),
             inc="act")
        r.on("tensor", lambda t, kbv=kbv, i=i: t.matmul(
            ps_O[:, 0:qw], vt[:, kbv * 65:kbv * 65 + 65],
            pt[:, 0:qw], start=(i == 0), stop=(i == n - 1)), inc="pe")



def build_ffnA(pre_ln):
    nc = _new_nc()
    r = Rec(nc)
    KW = 2560
    HBW = 2561

    src = _din(nc, "src", (D, T))
    srch = _din(nc, "srch", (D, HALO))
    wqkv_i = _din(nc, "wqkv", (D, 1536))
    wo_i = _din(nc, "wo", (512, D))
    qs_i = _din(nc, "qs", (128, 1))
    ks_i = _din(nc, "ks", (128, 1))
    mW0_i = _din(nc, "mW0", (128, 4 * 512))
    ffq1_i = _din(nc, "ffq1", (64, T))
    ffq2_i = _din(nc, "ffq2", (64, T))
    ffka1_i = _din(nc, "ffka1", (64, KW))
    ffka2_i = _din(nc, "ffka2", (64, KW))
    ffkb1_i = _din(nc, "ffkb1", (64, KW))
    ffkb2_i = _din(nc, "ffkb2", (64, KW))
    umid_o = _dout(nc, "umid", (D, T))

    U = _sb(nc, "U", [128, 8 * T], F32)
    HB = _sb(nc, "HBs", [128, 8 * HBW], BF16)
    WQ = _sb(nc, "WQs", [128, 8 * 384], BF16)
    WO = _sb(nc, "WOs", [128, 4 * 128], BF16)
    QS = _sb(nc, "QSs", [128, 1], F32)
    KS = _sb(nc, "KSs", [128, 1], F32)
    MW0 = _sb(nc, "MW0", [128, 4 * 512], BF16)
    MKL = _sb(nc, "MKL", [128, 8 * 512], BF16)
    FQ1 = _sb(nc, "FQ1", [64, T], BF16)
    FQ2 = _sb(nc, "FQ2", [64, T], BF16)
    FKA1 = _sb(nc, "FKA1", [64, KW], BF16)
    FKA2 = _sb(nc, "FKA2", [64, KW], BF16)
    FKB1 = _sb(nc, "FKB1", [64, KW], BF16)
    FKB2 = _sb(nc, "FKB2", [64, KW], BF16)
    with nc.Block() as block:
        kit = LNKit(nc)
        tiles, caps = load_consts(r, nc, ["ones128", "ones1x128", "head2",
                                          "head2exp"])
        cst = consts()
        kpre_ap = _din(nc, "c_kpre", cst["kpre"].shape, F32)
        ident_ap = _din(nc, "c_ident", cst["ident"].shape, F32)
        maskL_ap = _din(nc, "c_maskL", cst["maskL"].shape, F32)

        uh_guard = nc.sbuf_tensor("UH", [128, 8 * HALO], F32)
        UH = uh_guard.__enter__()

        r.dma("sync", U[:].rearrange("p (c w) -> p c w", w=T),
              src[:].rearrange("(c p) w -> p c w", p=128))
        r.dma("sync", UH[:].rearrange("p (c w) -> p c w", w=HALO),
              srch[:].rearrange("(c p) w -> p c w", p=128))
        r.dma("sync", QS[:], qs_i[:])
        r.dma("sync", KS[:], ks_i[:])
        r.dma("gpsimd", MW0[:], mW0_i[:])
        r.dma("gpsimd", MKL[:], maskL_ap[:])
        r.dma("gpsimd", FQ1[:], ffq1_i[:])
        r.dma("gpsimd", FQ2[:], ffq2_i[:])
        r.dma("gpsimd", FKA1[:], ffka1_i[:])
        r.dma("gpsimd", FKA2[:], ffka2_i[:])
        r.dma("gpsimd", FKB1[:], ffkb1_i[:])
        r.dma("gpsimd", FKB2[:], ffkb2_i[:])

        if pre_ln:
            rec_ln_stream(r, kit, tiles, U, T, T)
            rec_ln_stream(r, kit, tiles, UH, HALO, HALO)

        def hb_fn(v):
            for c in range(8):
                v.tensor_copy(HB[:, c * HBW:c * HBW + HALO], UH[:, c * HALO:(c + 1) * HALO])
                li = v.tensor_copy(HB[:, c * HBW + HALO:(c + 1) * HBW], U[:, c * T:(c + 1) * T])
            return li
        r.on("vector", hb_fn, inc="dve")

        uh_guard.__exit__(None, None, None)
        IDB = _sb(nc, "IDB", [128, 128], BF16)
        KTw = [_sb(nc, f"KTw{i}", [65, 1024], BF16) for i in range(2)]
        QTw = [_sb(nc, f"QTw{i}", [65, 512], BF16) for i in range(2)]
        VT = [_sb(nc, f"VT{i}", [128, 20 * 65], BF16) for i in range(2)]
        OALL = [_sb(nc, f"OALL{i}", [128, T], BF16) for i in range(4)]
        PT = _sb(nc, "PTs", [128, 512], BF16)
        MXC = _sb(nc, "MXC", [128, 4], F32)
        MXB = _sb(nc, "MXB", [128, 4], BF16)
        MT = _sb(nc, "MTs", [4, 128], BF16)
        OSB = _sb(nc, "OSB", [64, 512], F32)
        RCP = _sb(nc, "RCPs", [1, 512], F32)

        ps_qkv = kit.ps_b0
        ps_n = kit.ps_s0
        ps_bc = kit.ps_b1
        ps_S = _psum(nc, "ps_S", [128, 640], F32)
        ps_O = _psum(nc, "ps_O", [65, 512], F32)
        ps_T = _psum(nc, "ps_T", [4, 128], BF16)

        r.dma("gpsimd", IDB[:], ident_ap[:])
        for i in range(2):
            r.dma("gpsimd", KTw[i][64:65, :], kpre_ap[0:1, 0:1024])
            r.on("gpsimd", lambda g, i=i: g.memset(
                VT[i][:].rearrange("p (b c) -> p b c", c=65)[:, :, 64:65], 1.0),
                inc="pool")

        def hslice(c, j0, w):
            off = 1 if c < 4 else 0
            return HB[:, c * HBW + j0 + off: c * HBW + j0 + off + w]

        for hp in range(4):
            r.dma("gpsimd", WQ[:].rearrange("p (c w) -> p c w", w=384)[:, :, 0:128],
                  wqkv_i[:, hp * 128:(hp + 1) * 128].rearrange("(c p) w -> p c w", p=128))
            r.dma("gpsimd", WQ[:].rearrange("p (c w) -> p c w", w=384)[:, :, 128:256],
                  wqkv_i[:, 512 + hp * 128:512 + (hp + 1) * 128].rearrange("(c p) w -> p c w", p=128))
            r.dma("gpsimd", WQ[:].rearrange("p (c w) -> p c w", w=384)[:, :, 256:384],
                  wqkv_i[:, 1024 + hp * 128:1024 + (hp + 1) * 128].rearrange("(c p) w -> p c w", p=128))

            for blk in range(20):
                j0 = blk * 128

                def vmm_fn(t, j0=j0):
                    for sub in range(2):
                        for c in range(8):
                            li = t.matmul(ps_qkv[0:128, sub * 64:sub * 64 + 64],
                                          hslice(c, j0, 128),
                                          WQ[:, c * 384 + 256 + sub * 64:c * 384 + 256 + sub * 64 + 64],
                                          start=(c == 0), stop=(c == 7))
                    return li
                r.on("tensor", vmm_fn, inc="pe")

                def vev_fn(s, blk=blk):
                    s.copy(VT[0][:, blk * 65:blk * 65 + 64], ps_qkv[0:128, 0:64])
                    return s.copy(VT[1][:, blk * 65:blk * 65 + 64], ps_qkv[0:128, 64:128])
                r.on("scalar", vev_fn, inc="act")

            for w in range(4):
                w0 = w * 512
                for half in range(2):
                    kj0 = w0 + half * 512

                    def kmm_fn(t, kj0=kj0):
                        for c in range(8):
                            li = t.matmul(ps_qkv[:], WQ[:, c * 384 + 128:c * 384 + 256],
                                          hslice(c, kj0, 512), start=(c == 0), stop=(c == 7))
                        return li
                    r.on("tensor", kmm_fn, inc="pe")
                    t1 = FKA1 if half == 1 else FKB1
                    t2 = FKA2 if half == 1 else FKB2
                    rec_norm_rope(r, tiles, kit, ps_qkv, ps_n, ps_bc, KS,
                                  t1, t2, kj0, 512,
                                  (KTw[0][0:64, half * 512:(half + 1) * 512],
                                   KTw[1][0:64, half * 512:(half + 1) * 512]))

                def qmm_fn(t, w0=w0):
                    for c in range(8):
                        li = t.matmul(ps_qkv[:], WQ[:, c * 384:c * 384 + 128],
                                      hslice(c, 512 + w0, 512), start=(c == 0), stop=(c == 7))
                    return li
                r.on("tensor", qmm_fn, inc="pe")
                rec_norm_rope(r, tiles, kit, ps_qkv, ps_n, ps_bc, QS,
                              FQ1, FQ2, w0, 512,
                              (QTw[0][0:64, :], QTw[1][0:64, :]))

                for sub in range(2):
                    Kt, Qt, Vt = KTw[sub], QTw[sub], VT[sub]
                    for qs in range(4):
                        def mx_fn(t, qs=qs, Kt=Kt, Qt=Qt):
                            t.matmul(ps_S[:, 0:512], Qt[0:64, qs * 128:(qs + 1) * 128],
                                     Kt[0:64, qs * 128:qs * 128 + 512], start=True, stop=True)
                            return t.matmul(ps_S[:, 512:640],
                                            Qt[0:64, qs * 128:(qs + 1) * 128],
                                            Kt[0:64, qs * 128 + 512:qs * 128 + 640],
                                            start=True, stop=True)
                        r.on("tensor", mx_fn, inc="pe")
                        r.on("vector", lambda v, qs=qs: v.reduce_max(
                            MXC[:, qs:qs + 1], ps_S[:, 0:640], AX), inc="dve")
                    r.on("vector", lambda v: v.tensor_scalar_mul(MXB[:], MXC[:], -1.0),
                         inc="dve")
                    r.on("tensor", lambda t: t.transpose(ps_T[:], MXB[:], IDB[:]),
                         inc="pe")
                    r.on("scalar", lambda s: s.copy(MT[:], ps_T[:]), inc="act")
                    for qs in range(4):
                        r.dma("scalar", Qt[64:65, qs * 128:(qs + 1) * 128],
                              MT[qs:qs + 1, :])
                    blocks = []
                    for m in range(8):
                        if w == 0 and m < 4:
                            msk = MW0[:, m * 512:(m + 1) * 512]
                        else:
                            msk = MKL[:, m * 512:(m + 1) * 512]
                        blocks.append((m, 4 * w + m, msk))
                    rec_attn_win(r, Kt, 65, Qt, Vt, PT, ps_S, ps_O, 512, blocks)
                    hh = 2 * hp + sub
                    dst = OALL[hh // 2][(hh % 2) * 64:(hh % 2) * 64 + 64, w0:w0 + 512]
                    rec_o_norm(r, tiles, kit, ps_O, ps_bc, OSB, RCP, dst, 512)

        for f in range(8):
            r.dma("gpsimd", WO[:].rearrange("p (c w) -> p c w", w=128),
                  wo_i[:, f * 128:(f + 1) * 128].rearrange("(c p) w -> p c w", p=128))
            for tt_ in range(4):
                j0 = tt_ * 512

                def wo_fn(t, f=f, j0=j0):
                    for c in range(4):
                        li = t.matmul(ps_qkv[:], WO[:, c * 128:(c + 1) * 128],
                                      OALL[c][:, j0:j0 + 512], start=(c == 0), stop=(c == 3))
                    return li
                r.on("tensor", wo_fn, inc="pe")
                r.on("vector", lambda v, f=f, j0=j0: v.tensor_tensor(
                    U[:, f * T + j0:f * T + j0 + 512], U[:, f * T + j0:f * T + j0 + 512],
                    ps_qkv[:], ALU.add), inc="dve")

        r.dma("sync", umid_o[:].rearrange("(c p) w -> p c w", p=128),
              U[:].rearrange("p (c w) -> p c w", w=T), inc="st")
        r.on("sync", lambda e: e.nop(), waits=r.snap())
        r.finalize(block)
    return nc


def ffnA_inputs(stream_fm, c, layer, W, pre_ln):
    cst = consts()
    b, t0 = core_pos(c)

    def ksl(name):
        out = np.zeros((64, 2560), np.float32)
        lo = t0 - 512
        s = max(lo, 0)
        out[:, s - lo:] = cst[name][:, s:t0 + T]
        return out

    m = {
        "src": _slice_own(stream_fm, c),
        "srch": _slice_halo(stream_fm, c),
        "wqkv": np.ascontiguousarray(W["L0_ffn_wqkv"][layer]),
        "wo": np.ascontiguousarray(W["L0_ffn_wo"][layer]),
        "qs": np.tile(np.asarray(W["L0_ffn_q_scale"][layer], np.float32).reshape(64, 1), (2, 1)),
        "ks": np.tile(np.asarray(W["L0_ffn_k_scale"][layer], np.float32).reshape(64, 1), (2, 1)),
        "mW0": cst["maskL0"] if t0 == 0 else np.ascontiguousarray(cst["maskL"][:, 0:4 * 512]),
        "ffq1": np.ascontiguousarray(cst["ffq_t1"][:, t0:t0 + T]),
        "ffq2": np.ascontiguousarray(cst["ffq_t2"][:, t0:t0 + T]),
        "ffka1": ksl("ffka_t1"),
        "ffka2": ksl("ffka_t2"),
        "ffkb1": ksl("ffkb_t1"),
        "ffkb2": ksl("ffkb_t2"),
    }
    m.update(const_in_map(["ones128", "ones1x128", "head2", "head2exp",
                           "kpre", "ident", "maskL"]))
    return m



def build_ffnB(final_x):
    nc = _new_nc()
    r = Rec(nc)
    UW = T + 1

    u_i = _din(nc, "u", (D, T))
    up_i = _din(nc, "uprev", (D, 1))
    lng_i = _din(nc, "lng", (128, 8))
    lnb_i = _din(nc, "lnb", (128, 8))
    w1_i = _din(nc, "w1", (D, 5460))
    b1a_i = _din(nc, "b1a", (128, 22))
    b1g_i = _din(nc, "b1g", (128, 22))
    w2_i = _din(nc, "w2", (2730, D))
    b2_i = _din(nc, "b2", (128, 8))
    if final_x:
        x_i = _din(nc, "x", (D, T))
        out_o = _dout(nc, "xout", (D, T))
    else:
        out_o = _dout(nc, "uout", (D, T))

    U = _sb(nc, "U", [128, 8 * UW], F32)
    Z = _sb(nc, "Z", [128, 8 * 512], BF16)
    HT = _sb(nc, "HT", [128, 22 * 512], BF16)
    W1C = _sb(nc, "W1C", [128, 8 * 256], BF16)
    W2C = _sb(nc, "W2C", [128, 22 * 128], BF16)
    GEL = _sb(nc, "GEL", [128, 512], BF16)
    LNG = _sb(nc, "LNG", [128, 8], F32)
    LNB = _sb(nc, "LNB", [128, 8], F32)
    B1A = _sb(nc, "B1A", [128, 22], F32)
    B1G = _sb(nc, "B1G", [128, 22], F32)
    B2 = _sb(nc, "B2s", [128, 8], F32)
    if final_x:
        X = _sb(nc, "X", [128, 8 * T], F32)

    ps_a = _psum(nc, "ps_a", [128, 512], F32)
    ps_g = _psum(nc, "ps_g", [128, 512], F32)
    ps_w2 = _psum(nc, "ps_w2", [128, 512], F32)

    with nc.Block() as block:
        kit = LNKit(nc)
        tiles, caps = load_consts(r, nc, ["ones128", "ones1x128"])

        r.dma("sync", U[:].rearrange("p (c w) -> p c w", w=UW)[:, :, 1:UW],
              u_i[:].rearrange("(c p) w -> p c w", p=128))
        r.dma("sync", U[:].rearrange("p (c w) -> p c w", w=UW)[:, :, 0:1],
              up_i[:].rearrange("(c p) w -> p c w", p=128), slow=True)
        r.dma("sync", LNG[:], lng_i[:])
        r.dma("sync", LNB[:], lnb_i[:])
        r.dma("sync", B1A[:], b1a_i[:])
        r.dma("sync", B1G[:], b1g_i[:])
        r.dma("sync", B2[:], b2_i[:])
        if final_x:
            r.dma("sync", X[:].rearrange("p (c w) -> p c w", w=T),
                  x_i[:].rearrange("(c p) w -> p c w", p=128))

        for tt_ in reversed(range(4)):
            tb = tt_ * 512

            def sl_in(c, j0, w, tb=tb):
                off = 1 if c < 4 else 0
                return U[:, c * UW + off + tb + j0: c * UW + off + tb + j0 + w]

            def sl_out(c, j0, w):
                return Z[:, c * 512 + j0: c * 512 + j0 + w]

            rec_ln(r, kit, tiles, sl_in, sl_out, 512, gb=(LNG, LNB))

            for jc in range(22):
                acw = 128 if jc < 21 else 42
                r.dma("gpsimd", W1C[:].rearrange("p (c w) -> p c w", w=256)[:, :, 0:acw],
                      w1_i[:, jc * 128:jc * 128 + acw].rearrange("(c p) w -> p c w", p=128))
                r.dma("gpsimd", W1C[:].rearrange("p (c w) -> p c w", w=256)[:, :, 128:128 + acw],
                      w1_i[:, 2730 + jc * 128:2730 + jc * 128 + acw].rearrange("(c p) w -> p c w", p=128))

                def ag_fn(t, jc=jc, acw=acw):
                    for c in range(8):
                        t.matmul(ps_a[0:acw, :], W1C[:, c * 256:c * 256 + acw],
                                 Z[:, c * 512:(c + 1) * 512],
                                 start=(c == 0), stop=(c == 7))
                    for c in range(8):
                        li = t.matmul(ps_g[0:acw, :], W1C[:, c * 256 + 128:c * 256 + 128 + acw],
                                      Z[:, c * 512:(c + 1) * 512],
                                      start=(c == 0), stop=(c == 7))
                    return li
                r.on("tensor", ag_fn, inc="pe")
                r.on("scalar", lambda s, jc=jc, acw=acw: s.activation(
                    GEL[0:acw, :], ps_g[0:acw, :], AF.Gelu, bias=B1G[0:acw, jc:jc + 1]),
                    inc="act")
                r.on("vector", lambda v, jc=jc, acw=acw: v.scalar_tensor_tensor(
                    HT[0:acw, jc * 512:(jc + 1) * 512], ps_a[0:acw, :],
                    B1A[0:acw, jc:jc + 1], GEL[0:acw, :], ALU.add, ALU.mult),
                    inc="dve")

            for f in range(8):
                r.dma("gpsimd", W2C[:, 0:21 * 128],
                      w2_i[0:21 * 128, f * 128:(f + 1) * 128].rearrange("(c p) w -> p c w", p=128))
                r.dma("gpsimd", W2C[0:42, 21 * 128:22 * 128],
                      w2_i[21 * 128:2730, f * 128:(f + 1) * 128])

                def w2_fn(t, f=f):
                    for c in range(22):
                        kc = 128 if c < 21 else 42
                        li = t.matmul(ps_w2[:], W2C[0:kc, c * 128:(c + 1) * 128],
                                      HT[0:kc, c * 512:(c + 1) * 512],
                                      start=(c == 0), stop=(c == 21))
                    return li
                r.on("tensor", w2_fn, inc="pe")
                r.on("vector", lambda v, f=f, tb=tb: v.scalar_tensor_tensor(
                    U[:, f * UW + 1 + tb:f * UW + 1 + tb + 512], ps_w2[:],
                    B2[:, f:f + 1], U[:, f * UW + 1 + tb:f * UW + 1 + tb + 512],
                    ALU.add, ALU.add), inc="dve")

        if final_x:
            def xres_fn(v):
                for f in range(8):
                    li = v.scalar_tensor_tensor(
                        X[:, f * T:(f + 1) * T], U[:, f * UW + 1:f * UW + 1 + T],
                        0.5, X[:, f * T:(f + 1) * T], ALU.mult, ALU.add)
                return li
            r.on("vector", xres_fn, inc="dve")
            r.dma("sync", out_o[:].rearrange("(c p) w -> p c w", p=128),
                  X[:].rearrange("p (c w) -> p c w", w=T), inc="st")
        else:
            r.dma("sync", out_o[:].rearrange("(c p) w -> p c w", p=128),
                  U[:].rearrange("p (c w) -> p c w", w=UW)[:, :, 1:UW], inc="st")
        r.on("sync", lambda e: e.nop(), waits=r.snap())
        r.finalize(block)
    return nc


def ffnB_inputs(u_fm, c, layer, W, x_fm=None):
    m = {
        "u": _slice_own(u_fm, c),
        "uprev": _slice_halo(u_fm, c, 1),
        "lng": _pp(W["L0_ffn_ln_g"][layer]),
        "lnb": _pp(W["L0_ffn_ln_b"][layer]),
        "w1": np.ascontiguousarray(W["L0_ffn_w1"][layer]),
        "b1a": _pp_pad(W["L0_ffn_b1"][layer][:2730], 22, 2730),
        "b1g": _pp_pad(W["L0_ffn_b1"][layer][2730:], 22, 2730),
        "w2": np.ascontiguousarray(W["L0_ffn_w2"][layer]),
        "b2": _pp(W["L0_ffn_b2"][layer]),
    }
    if x_fm is not None:
        m["x"] = _slice_own(x_fm, c)
    m.update(const_in_map(["ones128", "ones1x128"]))
    return m



def prep_weights(inputs):
    W = {}
    for k, v in inputs.items():
        W[k] = np.asarray(v, np.float32)
    wq = W["L0_attn_w_qkv"].copy()
    ht = np.arange(16) % 4
    perm64 = np.concatenate([np.arange(0, 64, 2), np.arange(1, 64, 2)])
    for h in range(16):
        if (ht[h] >> 1) & 1:
            for base in (0, 1024):
                cols = base + h * 64 + np.arange(64)
                wq[:, cols] = wq[:, cols[perm64]]
    W["att_wqkv_perm"] = np.ascontiguousarray(wq)
    for k in ["L1_attn_q_w", "L1_attn_k_w", "L1_attn_v_w", "L1_attn_o_w",
              "L1_attn_fc_w", "L1_attn_proj_w", "L1_ffn_w1_w", "L1_ffn_w2_w",
              "L1_ffn_h1_w", "L1_ffn_h2_w", "L1_ffn_w3_w"]:
        W[k + "_t"] = np.ascontiguousarray(W[k].T)
    return W


_NC_CACHE = {}


def _nc_get(key, fn):
    if key not in _NC_CACHE:
        _NC_CACHE[key] = fn()
    return _NC_CACHE[key]


def _run(nc, in_maps):
    return run_bass_kernel_spmd(nc, in_maps, core_ids=list(range(NCORES)))


def _ffn_phase(x_fm, W):
    ncA0 = _nc_get("A0", lambda: build_ffnA(pre_ln=True))
    ncA1 = _nc_get("A1", lambda: build_ffnA(pre_ln=False))
    ncB0 = _nc_get("B0", lambda: build_ffnB(final_x=False))
    ncB1 = _nc_get("B1", lambda: build_ffnB(final_x=True))
    res = _run(ncA0, [ffnA_inputs(x_fm, c, 0, W, True) for c in range(NCORES)])
    m_fm = _assemble_fm([res.results[c]["umid"] for c in range(NCORES)])
    res = _run(ncB0, [ffnB_inputs(m_fm, c, 0, W) for c in range(NCORES)])
    u1_fm = _assemble_fm([res.results[c]["uout"] for c in range(NCORES)])
    res = _run(ncA1, [ffnA_inputs(u1_fm, c, 1, W, False) for c in range(NCORES)])
    m2_fm = _assemble_fm([res.results[c]["umid"] for c in range(NCORES)])
    res = _run(ncB1, [ffnB_inputs(m2_fm, c, 1, W, x_fm=x_fm) for c in range(NCORES)])
    return _assemble_fm([res.results[c]["xout"] for c in range(NCORES)])


def _attn_phase(x_fm, W):
    nc1 = _nc_get("Q", build_attn_qkv)
    nc2 = _nc_get("C", build_attn_core)
    nc3 = _nc_get("P", build_attn_proj)
    res = _run(nc1, [attn_qkv_inputs(x_fm, c, W) for c in range(NCORES)])
    qf = np.empty((B, D, SEQ), BF)
    kf = np.empty((B, D, SEQ), BF)
    vf = np.empty((B, SEQ, D), BF)
    for c in range(NCORES):
        b, t0 = core_pos(c)
        qf[b, :, t0:t0 + T] = res.results[c]["q_o"]
        kf[b, :, t0:t0 + T] = res.results[c]["k_o"]
        vf[b, t0:t0 + T, :] = res.results[c]["v_o"]
    chif = np.empty((16, SEQ), BF)
    clof = np.empty((16, SEQ), BF)
    for c in range(4):
        b, t0 = core_pos(c)
        chif[:, t0:t0 + T] = res.results[c]["chi"]
        clof[:, t0:t0 + T] = res.results[c]["clo"]
    res2 = _run(nc2, [attn_core_inputs(qf, kf, vf, chif, clof, c, W)
                      for c in range(NCORES)])
    o_fm = np.empty((B, D, SEQ), BF)
    for c in range(NCORES):
        b = c // 4
        h0 = (c % 4) * 4
        o_fm[b, h0 * 64:(h0 + 4) * 64, :] = res2.results[c]["o4"]
    res3 = _run(nc3, [attn_proj_inputs(x_fm, o_fm, c, W) for c in range(NCORES)])
    return _assemble_fm([res3.results[c]["xout"] for c in range(NCORES)])


def kernel(**inputs):
    W = prep_weights(inputs)
    x_fm = _fm(inputs["L0_attn_x"])
    x_fm = _ffn_phase(x_fm, W)
    x_fm = _attn_phase(x_fm, W)
    x_fm = _ffn_phase(x_fm, W)
    ncF = _nc_get("F", build_l1ffn)
    ncL = _nc_get("L", build_l1attn)
    res = _run(ncF, [l1ffn_inputs(x_fm, c, W) for c in range(NCORES)])
    x_fm = _assemble_fm([res.results[c]["xout"] for c in range(NCORES)])
    res = _run(ncL, [l1attn_inputs(x_fm, c, W) for c in range(NCORES)])
    x_fm = _assemble_fm([res.results[c]["xout"] for c in range(NCORES)])
    res = _run(ncF, [l1ffn_inputs(x_fm, c, W) for c in range(NCORES)])
    x_fm = _assemble_fm([res.results[c]["xout"] for c in range(NCORES)])
    return np.ascontiguousarray(np.transpose(x_fm, (0, 2, 1)))



def build_attn_qkv():
    nc = _new_nc()
    r = Rec(nc)

    x_i = _din(nc, "x", (D, T))
    wq_i = _din(nc, "wqkv", (D, 3072))
    lx_i = _din(nc, "logxi", (16, 1))
    eb_i = _din(nc, "ebits", (16, 1))
    j16_i = _din(nc, "j16", (16, T))
    aq1_i = _din(nc, "aq1", (64, T))
    aq2_i = _din(nc, "aq2", (64, T))
    ak1_i = _din(nc, "ak1", (64, T))
    ak2_i = _din(nc, "ak2", (64, T))
    q_o = _dout(nc, "q_o", (D, T), BF16)
    k_o = _dout(nc, "k_o", (D, T), BF16)
    v_o = _dout(nc, "v_o", (T, D), BF16)
    chi_o = _dout(nc, "chi", (16, T), BF16)
    clo_o = _dout(nc, "clo", (16, T), BF16)

    X = _sb(nc, "X", [128, 8 * T], F32)
    Z = _sb(nc, "Z", [128, 8 * T], BF16)
    WC = _sb(nc, "WC", [128, 8 * 512], BF16)
    STG = _sb(nc, "STG", [128, 512], BF16)
    AQ1 = _sb(nc, "AQ1", [64, T], BF16)
    AQ2 = _sb(nc, "AQ2", [64, T], BF16)
    AK1 = _sb(nc, "AK1", [64, T], BF16)
    AK2 = _sb(nc, "AK2", [64, T], BF16)
    LX = _sb(nc, "LX", [16, 1], F32)
    EB = _sb(nc, "EB", [16, 1], F32)
    J16 = _sb(nc, "J16", [16, T], F32)
    C32 = _sb(nc, "C32", [16, 2 * T], F32)
    CBF = _sb(nc, "CBF", [16, 2 * T], BF16)

    ht = np.arange(16) % 4
    PI = [(int(h) >> 1) & 1 for h in ht]

    with nc.Block() as block:
        kit = LNKit(nc)
        tiles, caps = load_consts(r, nc, ["ones128", "ones1x128"])
        ps_mm = kit.ps_b0
        ps_n = kit.ps_s0

        r.dma("sync", X[:].rearrange("p (c w) -> p c w", w=T),
              x_i[:].rearrange("(c p) w -> p c w", p=128))
        r.dma("gpsimd", AQ1[:], aq1_i[:])
        r.dma("gpsimd", AQ2[:], aq2_i[:])
        r.dma("gpsimd", AK1[:], ak1_i[:])
        r.dma("gpsimd", AK2[:], ak2_i[:])
        r.dma("sync", LX[:], lx_i[:])
        r.dma("sync", EB[:], eb_i[:])
        r.dma("sync", J16[:], j16_i[:])

        rec_ln_stream(r, kit, tiles, X, T, T, out=Z, out_chnk=T)

        r.on("scalar", lambda s: s.activation(C32[0:16, 0:1], LX[:], AF.Exp,
                                              scale=-1.0), inc="act")
        r.on("vector", lambda v: v.tensor_tensor(C32[0:16, 1:2], C32[0:16, 0:1],
                                                 EB[:], ALU.mult), inc="dve")

        def crow_fn(v):
            v.tensor_scalar_mul(C32[0:16, T:2 * T], J16[:], C32[0:16, 1:2])
            v.tensor_scalar_mul(CBF[0:16, 0:T], C32[0:16, T:2 * T], 1.0)
            v.tensor_scalar_mul(C32[0:16, 0:T], CBF[0:16, 0:T], 1.0)
            v.tensor_tensor(C32[0:16, 0:T], C32[0:16, T:2 * T], C32[0:16, 0:T],
                            ALU.subtract)
            return v.tensor_scalar_mul(CBF[0:16, T:2 * T], C32[0:16, 0:T], 1.0)
        r.on("vector", crow_fn, inc="dve")
        r.dma("sync", chi_o[:], CBF[0:16, 0:T], inc="st")
        r.dma("sync", clo_o[:], CBF[0:16, T:2 * T], inc="st")

        for role in range(2):
            base = role * 1024
            t1 = AQ1 if role == 0 else AK1
            t2 = AQ2 if role == 0 else AK2
            out_ap = q_o if role == 0 else k_o
            for oc in range(8):
                r.dma("gpsimd", WC[:, 0:8 * 128].rearrange("p (c w) -> p c w", w=128),
                      wq_i[:, base + oc * 128:base + (oc + 1) * 128]
                      .rearrange("(c p) w -> p c w", p=128))
                for it in range(4):
                    j0 = it * 512

                    def mm_fn(t, j0=j0):
                        for c in range(8):
                            li = t.matmul(ps_mm[:], WC[:, c * 128:(c + 1) * 128],
                                          Z[:, c * T + j0:c * T + j0 + 512],
                                          start=(c == 0), stop=(c == 7))
                        return li
                    r.on("tensor", mm_fn, inc="pe")
                    h0, h1 = 2 * oc, 2 * oc + 1
                    scl = 0.125 if role == 0 else 1.0
                    if PI[h0]:
                        rec_rope_plain(r, kit, ps_mm, t1, t2, j0, 512,
                                       (STG[0:64, :], STG[64:128, :]), True, scl)
                    else:
                        rec_rope_plain(r, kit, ps_mm, t1, t2, j0, 512,
                                       (STG[0:64, :], STG[64:128, :]), False, scl)
                    r.dma("sync", out_ap[oc * 128:(oc + 1) * 128, j0:j0 + 512],
                          STG[:], inc="st")


        for vh in range(2):
            r.dma("gpsimd", WC[:].rearrange("p (c w) -> p c w", w=512),
                  wq_i[:, 2048 + vh * 512:2048 + (vh + 1) * 512]
                  .rearrange("(c p) w -> p c w", p=128))
            for tc in range(16):
                def vmm_fn(t, tc=tc):
                    for c in range(8):
                        li = t.matmul(ps_mm[:], Z[:, c * T + tc * 128:c * T + (tc + 1) * 128],
                                      WC[:, c * 512:(c + 1) * 512],
                                      start=(c == 0), stop=(c == 7))
                    return li
                r.on("tensor", vmm_fn, inc="pe")
                r.on("scalar", lambda s: s.copy(STG[:], ps_mm[:]), inc="act")
                r.dma("sync", v_o[tc * 128:(tc + 1) * 128, vh * 512:(vh + 1) * 512],
                      STG[:], inc="st")

        r.on("sync", lambda e: e.nop(), waits=r.snap())
        r.finalize(block)
    return nc


def attn_qkv_inputs(x_fm, c, W):
    cst = consts()
    b, t0 = core_pos(c)
    m = {
        "x": _slice_own(x_fm, c),
        "wqkv": W["att_wqkv_perm"],
        "logxi": np.asarray(W["L0_attn_log_xi"], np.float32).reshape(16, 1),
        "ebits": cst["ebits"],
        "j16": np.ascontiguousarray(cst["j16"][:, t0:t0 + T]),
        "aq1": np.ascontiguousarray(cst["att_t1q"][:, t0:t0 + T]),
        "aq2": np.ascontiguousarray(cst["att_t2q"][:, t0:t0 + T]),
        "ak1": np.ascontiguousarray(cst["att_t1"][:, t0:t0 + T]),
        "ak2": np.ascontiguousarray(cst["att_t2"][:, t0:t0 + T]),
    }
    m.update(const_in_map(["ones128", "ones1x128"]))
    return m



def build_attn_core():
    nc = _new_nc()
    r = Rec(nc)

    q4_i = _din(nc, "q4", (256, SEQ), BF16)
    k4_i = _din(nc, "k4", (256, SEQ), BF16)
    v4_i = _din(nc, "v4", (4 * SEQ, 64), BF16)
    chi_i = _din(nc, "chi4", (4, SEQ), BF16)
    clo_i = _din(nc, "clo4", (4, SEQ), BF16)
    lxs_i = _din(nc, "logxi4", (1, 4))
    ebs_i = _din(nc, "ebits4", (1, 4))
    jr_i = _din(nc, "jrow", (1, SEQ))
    o_o = _dout(nc, "o4", (256, SEQ), BF16)

    KT = _sb(nc, "KT", [68, SEQ], BF16)
    QT = _sb(nc, "QT", [68, SEQ], BF16)
    VT = _sb(nc, "VT", [128, 64 * 65], BF16)
    PT = _sb(nc, "PTs", [128, 1024], BF16)
    MC = _sb(nc, "MC", [128, 5 * 512], F32)
    JR = _sb(nc, "JR", [1, SEQ], F32)
    EXI = _sb(nc, "EXI", [1, 8], F32)
    SC = _sb(nc, "SCs", [64, 512], F32)
    ST = _sb(nc, "STs", [1, 4 * 512], F32)
    KMX = _sb(nc, "KMX", [1, 64], F32)
    OSB = _sb(nc, "OSB", [64, 1024], F32)
    RCP = _sb(nc, "RCPs", [1, 1024], F32)
    OST = _sb(nc, "OST", [64, 1024], BF16)

    ps_S = _psum(nc, "ps_S", [128, 1024], F32)
    ps_O = _psum(nc, "ps_O", [65, 1024], F32)
    ps_n = _psum(nc, "ps_n", [1, 512], F32)
    ps_bc = _psum(nc, "ps_bc", [64, 1024], F32)

    cst = consts()

    with nc.Block() as block:
        tiles, caps = load_consts(r, nc, ["ones128", "ones1x128"])
        kpre_ap = _din(nc, "c_kpre", cst["kpre"].shape, F32)
        qpre_ap = _din(nc, "c_qpre", cst["qpre"].shape, F32)
        maskC_ap = _din(nc, "c_maskC", cst["maskC"].shape, F32)

        r.dma("sync", MC[:], maskC_ap[:])
        r.dma("sync", JR[:], jr_i[:])
        EXL = _sb(nc, "EXL", [1, 8], F32)
        r.dma("sync", EXL[0:1, 0:4], lxs_i[:])
        r.dma("sync", EXL[0:1, 4:8], ebs_i[:])
        r.on("scalar", lambda s: s.activation(EXI[0:1, 0:4], EXL[0:1, 0:4], AF.Exp,
                                              scale=-1.0), inc="act")
        r.on("vector", lambda v: v.tensor_tensor(EXI[0:1, 4:8], EXI[0:1, 0:4],
                                                 EXL[0:1, 4:8], ALU.mult), inc="dve")
        r.on("gpsimd", lambda g: g.memset(
            VT[:].rearrange("p (b c) -> p b c", c=65)[:, :, 64:65], 1.0), inc="pool")

        for p_ in range(4):
            r.dma("sync", KT[0:64, :], k4_i[p_ * 64:(p_ + 1) * 64, :])
            r.dma("gpsimd", KT[64:65, :], kpre_ap[0:1, :])
            r.dma("sync", KT[65:66, :], chi_i[p_:p_ + 1, :])
            r.dma("sync", KT[66:67, :], clo_i[p_:p_ + 1, :])
            r.dma("gpsimd", KT[67:68, :], kpre_ap[3:4, :])
            r.dma("sync", QT[0:64, :], q4_i[p_ * 64:(p_ + 1) * 64, :])
            r.dma("gpsimd", QT[65:67, :], qpre_ap[0:2, :])
            r.dma("gpsimd", QT[67:68, :], qpre_ap[2:3, :])
            for kb in range(64):
                r.dma("sync", VT[:, kb * 65:kb * 65 + 64],
                      v4_i[p_ * SEQ + kb * 128:p_ * SEQ + (kb + 1) * 128, :])

            for it in range(16):
                r.on("scalar", lambda s, it=it: s.activation(
                    SC[:, 0:512], KT[0:64, it * 512:(it + 1) * 512], AF.Square),
                    inc="act")
                r.on("tensor", lambda t: t.matmul(
                    ps_n[0:1, 0:512], tiles["ones128"][0:64, :], SC[:, 0:512],
                    start=True, stop=True), inc="pe")
                r.on("vector", lambda v, it=it: v.reduce_max(
                    KMX[0:1, it:it + 1], ps_n[0:1, 0:512], AX), inc="dve")
            r.on("vector", lambda v: v.reduce_max(KMX[0:1, 32:33], KMX[0:1, 0:16], AX),
                 inc="dve")
            r.on("scalar", lambda s: s.activation(KMX[0:1, 33:34], KMX[0:1, 32:33],
                                                  AF.Sqrt), inc="act")

            for it in range(16):
                r.on("scalar", lambda s, it=it: s.activation(
                    SC[:, 0:512], QT[0:64, it * 512:(it + 1) * 512], AF.Square),
                    inc="act")
                r.on("tensor", lambda t: t.matmul(
                    ps_n[0:1, 0:512], tiles["ones128"][0:64, :], SC[:, 0:512],
                    start=True, stop=True), inc="pe")
                r.on("scalar", lambda s: s.activation(ST[0:1, 0:512],
                                                      ps_n[0:1, 0:512], AF.Sqrt),
                     inc="act")

                def brow_fn(v, it=it, p_=p_):
                    v.tensor_scalar_mul(ST[0:1, 512:1024], ST[0:1, 0:512],
                                        KMX[0:1, 33:34])
                    v.scalar_tensor_tensor(ST[0:1, 1024:1536],
                                           JR[0:1, it * 512:(it + 1) * 512],
                                           EXI[0:1, 4 + p_:5 + p_],
                                           ST[0:1, 512:1024], ALU.mult, ALU.add)
                    return v.tensor_scalar_mul(QT[64:65, it * 512:(it + 1) * 512],
                                               ST[0:1, 1024:1536], -1.0)
                r.on("vector", brow_fn, inc="dve")

            for qt2 in range(8):
                q0 = qt2 * 1024
                nb = 8 * qt2 + 8
                for kb in range(nb):
                    rel = kb - 8 * qt2

                    def s_fn(t, kb=kb, q0=q0):
                        t.matmul(ps_S[:, 0:512], KT[0:68, kb * 128:(kb + 1) * 128],
                                 QT[0:68, q0:q0 + 512], start=True, stop=True)
                        return t.matmul(ps_S[:, 512:1024],
                                        KT[0:68, kb * 128:(kb + 1) * 128],
                                        QT[0:68, q0 + 512:q0 + 1024],
                                        start=True, stop=True)
                    r.on("tensor", s_fn, inc="pe")
                    if rel >= 0:
                        def msk_fn(v, rel=rel):
                            li = None
                            if rel < 4:
                                li = v.tensor_tensor(ps_S[:, 0:512], ps_S[:, 0:512],
                                                     MC[:, rel * 512:(rel + 1) * 512],
                                                     ALU.add)
                            else:
                                v.tensor_tensor(ps_S[:, 0:512], ps_S[:, 0:512],
                                                MC[:, 4 * 512:5 * 512], ALU.add)
                                li = v.tensor_tensor(ps_S[:, 512:1024], ps_S[:, 512:1024],
                                                     MC[:, (rel - 4) * 512:(rel - 3) * 512],
                                                     ALU.add)
                            return li
                        r.on("vector", msk_fn, inc="dve")
                    r.on("scalar", lambda s: s.activation(PT[:], ps_S[:], AF.Exp),
                         inc="act")

                    def pv_fn(t, kb=kb, first=(kb == 0), last=(kb == nb - 1)):
                        t.matmul(ps_O[:, 0:512], VT[:, kb * 65:kb * 65 + 65],
                                 PT[:, 0:512], start=first, stop=last)
                        return t.matmul(ps_O[:, 512:1024], VT[:, kb * 65:kb * 65 + 65],
                                        PT[:, 512:1024], start=first, stop=last)
                    r.on("tensor", pv_fn, inc="pe")

                r.on("vector", lambda v: v.reciprocal(RCP[0:1, :], ps_O[64:65, :]),
                     inc="dve")
                r.on("scalar", lambda s: s.copy(OSB[:], ps_O[0:64, :]), inc="act")
                def bc2_fn(t):
                    t.matmul(ps_bc[:, 0:512], tiles["ones1x128"][0:1, 0:64],
                             RCP[0:1, 0:512], start=True, stop=True)
                    return t.matmul(ps_bc[:, 512:1024], tiles["ones1x128"][0:1, 0:64],
                                    RCP[0:1, 512:1024], start=True, stop=True)
                r.on("tensor", bc2_fn, inc="pe")
                r.on("vector", lambda v: v.tensor_tensor(OST[:], OSB[:], ps_bc[:],
                                                         ALU.mult), inc="dve")
                r.dma("sync", o_o[p_ * 64:(p_ + 1) * 64, q0:q0 + 1024], OST[:],
                      inc="st")

        r.on("sync", lambda e: e.nop(), waits=r.snap())
        r.finalize(block)
    return nc


def attn_core_inputs(qf, kf, vf, chif, clof, c, W):
    cst = consts()
    b = c // 4
    h0 = (c % 4) * 4
    hs = list(range(h0, h0 + 4))
    q4 = np.concatenate([qf[b, h * 64:(h + 1) * 64, :] for h in hs], 0)
    k4 = np.concatenate([kf[b, h * 64:(h + 1) * 64, :] for h in hs], 0)
    v4 = np.concatenate([vf[b, :, h * 64:(h + 1) * 64] for h in hs], 0)
    ht = np.arange(16) % 4
    m = {
        "q4": np.ascontiguousarray(q4),
        "k4": np.ascontiguousarray(k4),
        "v4": np.ascontiguousarray(v4),
        "chi4": np.ascontiguousarray(chif[hs]),
        "clo4": np.ascontiguousarray(clof[hs]),
        "logxi4": np.asarray(W["L0_attn_log_xi"], np.float32).reshape(1, 16)[:, hs],
        "ebits4": np.ascontiguousarray(cst["ebits"].reshape(1, 16)[:, hs]),
        "jrow": cst["jrow"],
    }
    m.update(const_in_map(["ones128", "ones1x128", "kpre", "qpre", "maskC"]))
    return m



def build_attn_proj():
    nc = _new_nc()
    r = Rec(nc)

    x_i = _din(nc, "x", (D, T))
    o_i = _din(nc, "o", (D, T), BF16)
    wo_i = _din(nc, "wo", (D, D))
    xo = _dout(nc, "xout", (D, T))

    X = _sb(nc, "X", [128, 8 * T], F32)
    O = _sb(nc, "O", [128, 8 * T], BF16)
    WC = _sb(nc, "WC", [128, 8 * 128], BF16)
    ps = _psum(nc, "ps", [128, 512], F32)

    with nc.Block() as block:
        r.dma("sync", X[:].rearrange("p (c w) -> p c w", w=T),
              x_i[:].rearrange("(c p) w -> p c w", p=128))
        r.dma("sync", O[:].rearrange("p (c w) -> p c w", w=T),
              o_i[:].rearrange("(c p) w -> p c w", p=128))
        for f in range(8):
            r.dma("gpsimd", WC[:].rearrange("p (c w) -> p c w", w=128),
                  wo_i[:, f * 128:(f + 1) * 128].rearrange("(c p) w -> p c w", p=128))
            for it in range(4):
                j0 = it * 512

                def mm_fn(t, f=f, j0=j0):
                    for c in range(8):
                        li = t.matmul(ps[:], WC[:, c * 128:(c + 1) * 128],
                                      O[:, c * T + j0:c * T + j0 + 512],
                                      start=(c == 0), stop=(c == 7))
                    return li
                r.on("tensor", mm_fn, inc="pe")
                r.on("vector", lambda v, f=f, j0=j0: v.tensor_tensor(
                    X[:, f * T + j0:f * T + j0 + 512],
                    X[:, f * T + j0:f * T + j0 + 512], ps[:], ALU.add), inc="dve")

        r.dma("sync", xo[:].rearrange("(c p) w -> p c w", p=128),
              X[:].rearrange("p (c w) -> p c w", w=T), inc="st")
        r.on("sync", lambda e: e.nop(), waits=r.snap())
        r.finalize(block)
    return nc


def attn_proj_inputs(x_fm, o_fm, c, W):
    return {
        "x": _slice_own(x_fm, c),
        "o": _slice_own_bf(o_fm, c),
        "wo": W["L0_attn_w_o"],
    }


def _slice_own_bf(x_fm, c):
    b, t0 = core_pos(c)
    return np.ascontiguousarray(x_fm[b, :, t0:t0 + T])



def build_l1ffn():
    nc = _new_nc()
    r = Rec(nc)

    x_i = _din(nc, "x", (D, T))
    w1_i = _din(nc, "w1", (D, 4096))
    w2_i = _din(nc, "w2", (D, 4096))
    h1_i = _din(nc, "h1", (4096, 4096))
    h2_i = _din(nc, "h2", (4096, 4096))
    w3_i = _din(nc, "w3", (4096, D))
    s1_i = _din(nc, "s1", (128, 32))
    s2_i = _din(nc, "s2", (128, 32))
    sh1_i = _din(nc, "sh1", (128, 32))
    sh2_i = _din(nc, "sh2", (128, 32))
    s3_i = _din(nc, "s3", (128, 8))
    xo = _dout(nc, "xout", (D, T))

    X = _sb(nc, "X", [128, 8 * T], F32)
    Z = _sb(nc, "Z", [128, 8 * 512], BF16)
    HA = _sb(nc, "HA", [128, 32 * 512], BF16)
    HBB = _sb(nc, "HB2", [128, 32 * 512], BF16)
    WCA = _sb(nc, "WCA", [128, 8 * 128], BF16)
    WCB = _sb(nc, "WCB", [128, 8 * 128], BF16)
    WH = _sb(nc, "WH", [128, 32 * 128], BF16)
    SA = _sb(nc, "SA", [128, 512], BF16)
    S1 = _sb(nc, "S1", [128, 32], F32)
    S2 = _sb(nc, "S2", [128, 32], F32)
    SH1 = _sb(nc, "SH1", [128, 32], F32)
    SH2 = _sb(nc, "SH2", [128, 32], F32)
    S3H = _sb(nc, "S3H", [128, 8], F32)

    ps_a = _psum(nc, "ps_a", [128, 512], F32)
    ps_g = _psum(nc, "ps_g", [128, 512], F32)

    with nc.Block() as block:
        kit = LNKit(nc)
        tiles, caps = load_consts(r, nc, ["ones128", "ones1x128"])

        r.dma("sync", X[:].rearrange("p (c w) -> p c w", w=T),
              x_i[:].rearrange("(c p) w -> p c w", p=128))
        r.dma("sync", S1[:], s1_i[:])
        r.dma("sync", S2[:], s2_i[:])
        r.dma("sync", SH1[:], sh1_i[:])
        r.dma("sync", SH2[:], sh2_i[:])
        r.dma("sync", S3H[:], s3_i[:])
        r.on("vector", lambda v: v.tensor_scalar_mul(S3H[:], S3H[:], 0.5), inc="dve")

        for tt_ in range(4):
            tb = tt_ * 512

            def sl_in(c, j0, w, tb=tb):
                return X[:, c * T + tb + j0: c * T + tb + j0 + w]

            def sl_out(c, j0, w):
                return Z[:, c * 512 + j0: c * 512 + j0 + w]
            rec_ln(r, kit, tiles, sl_in, sl_out, 512)

            for oc in range(32):
                r.dma("gpsimd", WCA[:].rearrange("p (c w) -> p c w", w=128),
                      w1_i[:, oc * 128:(oc + 1) * 128].rearrange("(c p) w -> p c w", p=128))
                r.dma("gpsimd", WCB[:].rearrange("p (c w) -> p c w", w=128),
                      w2_i[:, oc * 128:(oc + 1) * 128].rearrange("(c p) w -> p c w", p=128))

                def mm1_fn(t):
                    for c in range(8):
                        t.matmul(ps_a[:], WCA[:, c * 128:(c + 1) * 128],
                                 Z[:, c * 512:(c + 1) * 512], start=(c == 0), stop=(c == 7))
                    for c in range(8):
                        li = t.matmul(ps_g[:], WCB[:, c * 128:(c + 1) * 128],
                                      Z[:, c * 512:(c + 1) * 512], start=(c == 0), stop=(c == 7))
                    return li
                r.on("tensor", mm1_fn, inc="pe")
                r.on("scalar", lambda s, oc=oc: s.activation(
                    SA[:], ps_a[:], AF.Silu, scale=S1[:, oc:oc + 1]), inc="act")
                r.on("vector", lambda v, oc=oc: v.scalar_tensor_tensor(
                    HA[:, oc * 512:(oc + 1) * 512], ps_g[:], S2[:, oc:oc + 1],
                    SA[:], ALU.mult, ALU.mult), inc="dve")

            for oc in range(32):
                r.dma("gpsimd", WH[:].rearrange("p (c w) -> p c w", w=128),
                      h1_i[:, oc * 128:(oc + 1) * 128].rearrange("(c p) w -> p c w", p=128))

                def mm2_fn(t):
                    for c in range(32):
                        li = t.matmul(ps_a[:], WH[:, c * 128:(c + 1) * 128],
                                      HA[:, c * 512:(c + 1) * 512],
                                      start=(c == 0), stop=(c == 31))
                    return li
                r.on("tensor", mm2_fn, inc="pe")
                r.on("scalar", lambda s, oc=oc: s.activation(
                    HBB[:, oc * 512:(oc + 1) * 512], ps_a[:], AF.Gelu,
                    scale=SH1[:, oc:oc + 1]), inc="act")

            for oc in range(32):
                r.dma("gpsimd", WH[:].rearrange("p (c w) -> p c w", w=128),
                      h2_i[:, oc * 128:(oc + 1) * 128].rearrange("(c p) w -> p c w", p=128))

                def mm3_fn(t):
                    for c in range(32):
                        li = t.matmul(ps_a[:], WH[:, c * 128:(c + 1) * 128],
                                      HBB[:, c * 512:(c + 1) * 512],
                                      start=(c == 0), stop=(c == 31))
                    return li
                r.on("tensor", mm3_fn, inc="pe")
                r.on("scalar", lambda s, oc=oc: s.activation(
                    HA[:, oc * 512:(oc + 1) * 512], ps_a[:], AF.Gelu,
                    scale=SH2[:, oc:oc + 1]), inc="act")

            for f in range(8):
                r.dma("gpsimd", WH[:, 0:32 * 128].rearrange("p (c w) -> p c w", w=128),
                      w3_i[:, f * 128:(f + 1) * 128].rearrange("(c p) w -> p c w", p=128))

                def mm4_fn(t):
                    for c in range(32):
                        li = t.matmul(ps_a[:], WH[:, c * 128:(c + 1) * 128],
                                      HA[:, c * 512:(c + 1) * 512],
                                      start=(c == 0), stop=(c == 31))
                    return li
                r.on("tensor", mm4_fn, inc="pe")
                r.on("vector", lambda v, f=f, tb=tb: v.scalar_tensor_tensor(
                    X[:, f * T + tb:f * T + tb + 512], ps_a[:], S3H[:, f:f + 1],
                    X[:, f * T + tb:f * T + tb + 512], ALU.mult, ALU.add), inc="dve")

        r.dma("sync", xo[:].rearrange("(c p) w -> p c w", p=128),
              X[:].rearrange("p (c w) -> p c w", w=T), inc="st")
        r.on("sync", lambda e: e.nop(), waits=r.snap())
        r.finalize(block)
    return nc


def l1ffn_inputs(x_fm, c, W):
    m = {
        "x": _slice_own(x_fm, c),
        "w1": W["L1_ffn_w1_w_t"],
        "w2": W["L1_ffn_w2_w_t"],
        "h1": W["L1_ffn_h1_w_t"],
        "h2": W["L1_ffn_h2_w_t"],
        "w3": W["L1_ffn_w3_w_t"],
        "s1": _pp(W["L1_ffn_w1_s"], 32),
        "s2": _pp(W["L1_ffn_w2_s"], 32),
        "sh1": _pp(W["L1_ffn_h1_s"], 32),
        "sh2": _pp(W["L1_ffn_h2_s"], 32),
        "s3": _pp(W["L1_ffn_w3_s"], 8),
    }
    m.update(const_in_map(["ones128", "ones1x128"]))
    return m



def build_l1attn():
    nc = _new_nc()
    r = Rec(nc)

    x_i = _din(nc, "x", (D, T))
    ln1g_i = _din(nc, "ln1g", (128, 8))
    ln1b_i = _din(nc, "ln1b", (128, 8))
    ln2g_i = _din(nc, "ln2g", (128, 8))
    ln2b_i = _din(nc, "ln2b", (128, 8))
    wq_i = _din(nc, "wq", (D, D))
    wk_i = _din(nc, "wk", (D, D))
    wv_i = _din(nc, "wv", (D, D))
    wo_i = _din(nc, "wo", (D, D))
    qb_i = _din(nc, "qb", (1, D))
    kb_i = _din(nc, "kb", (1, D))
    vb_i = _din(nc, "vb", (1, D))
    ob_i = _din(nc, "ob", (128, 8))
    fc_i = _din(nc, "fc", (D, 4096))
    fcb_i = _din(nc, "fcb", (128, 32))
    pj_i = _din(nc, "pj", (4096, D))
    pjb_i = _din(nc, "pjb", (128, 8))
    id_i = _din(nc, "c_ident", (128, 128))
    xo = _dout(nc, "xout", (D, T))

    X = _sb(nc, "X", [128, 8 * T], F32)
    ZARENA = _sb(nc, "ZARENA", [128, 20 * 1024], BF16)
    Z = ZARENA[:, 0:8 * T]
    ZT = ZARENA[:, 0:8 * 512]
    GT = ZARENA[:, 4096:4096 + 32 * 512]
    WAR = _sb(nc, "WAR", [128, 24 * 1024], BF16)
    OW = _sb(nc, "OW", [128, 8 * 128], BF16)
    LN1G = _sb(nc, "LN1G", [128, 8], F32)
    LN1B = _sb(nc, "LN1B", [128, 8], F32)
    LN2G = _sb(nc, "LN2G", [128, 8], F32)
    LN2B = _sb(nc, "LN2B", [128, 8], F32)
    OBP = _sb(nc, "OBP", [128, 8], F32)
    FCB = _sb(nc, "FCB", [128, 32], F32)
    PJB = _sb(nc, "PJB", [128, 8], F32)
    BB = _sb(nc, "BB", [128, 3 * D], F32)
    QTok = _sb(nc, "QTok", [128, D], BF16)
    KTok = _sb(nc, "KTok", [128, D], BF16)
    VTok = _sb(nc, "VTok", [128, D], BF16)
    PRD = _sb(nc, "PRD", [128, D], F32)
    AA = _sb(nc, "AA", [128, 256], F32)
    DN = _sb(nc, "DN", [128, 32], F32)
    OACC = _sb(nc, "OACC", [128, D], F32)
    ATok = _sb(nc, "ATok", [128, D], BF16)
    AFM = _sb(nc, "AFM", [128, D], BF16)
    IDB = _sb(nc, "IDB", [128, 128], BF16)

    ps_a = _psum(nc, "ps_a", [128, 512], F32)
    ps_tr = _psum(nc, "ps_tr", [128, 128], BF16)

    with nc.Block() as block:
        kit = LNKit(nc)
        tiles, caps = load_consts(r, nc, ["ones128", "ones1x128"])

        r.dma("sync", X[:].rearrange("p (c w) -> p c w", w=T),
              x_i[:].rearrange("(c p) w -> p c w", p=128))
        for tin, tsb in ((ln1g_i, LN1G), (ln1b_i, LN1B), (ln2g_i, LN2G),
                         (ln2b_i, LN2B), (ob_i, OBP), (fcb_i, FCB), (pjb_i, PJB)):
            r.dma("sync", tsb[:], tin[:])
        r.dma("sync", BB[:, 0:D], qb_i[:].broadcast_to([128, D]))
        r.dma("sync", BB[:, D:2 * D], kb_i[:].broadcast_to([128, D]))
        r.dma("sync", BB[:, 2 * D:3 * D], vb_i[:].broadcast_to([128, D]))
        r.dma("gpsimd", IDB[:], id_i[:])
        SCQ = kit.sq[:, 0:512]
        SCR = kit.sq[:, 512:1024]
        SCM = kit.sq[:, 1024:1536]

        rec_ln_stream(r, kit, tiles, X, T, T)

        def zaff_fn(v):
            for c in range(8):
                li = v.tensor_scalar(Z[:, c * T:(c + 1) * T], X[:, c * T:(c + 1) * T],
                                     LN1G[:, c:c + 1], LN1B[:, c:c + 1],
                                     ALU.mult, ALU.add)
            return li
        r.on("vector", zaff_fn, inc="dve")

        for wi, base in ((wq_i, 0), (wk_i, 8), (wv_i, 16)):
            r.dma("gpsimd", WAR[:, base * 1024:(base + 8) * 1024]
                  .rearrange("p (c w) -> p c w", w=1024),
                  wi[:].rearrange("(c p) w -> p c w", p=128))

        for tc in range(16):
            tb = tc * 128
            for pi_, (wbase, bseg, dst, do_elu) in enumerate(
                    ((0, 0, QTok, True), (8, 1, KTok, True), (16, 2, VTok, False))):
                for half in range(2):
                    def mm_fn(t, wbase=wbase, half=half, tb=tb):
                        for c in range(8):
                            li = t.matmul(ps_a[:],
                                          Z[:, c * T + tb:c * T + tb + 128],
                                          WAR[:, (wbase + c) * 1024 + half * 512:
                                              (wbase + c) * 1024 + (half + 1) * 512],
                                          start=(c == 0), stop=(c == 7))
                        return li
                    r.on("tensor", mm_fn, inc="pe")

                    def bias_fn(v, bseg=bseg, half=half):
                        return v.tensor_tensor(
                            SCQ[:], ps_a[:],
                            BB[:, bseg * 1024 + half * 512:bseg * 1024 + (half + 1) * 512],
                            ALU.add)
                    r.on("vector", bias_fn, inc="dve")
                    if do_elu:
                        def elu1_fn(v):
                            v.tensor_scalar_max(SCR[:], SCQ[:], 0.0)
                            return v.tensor_tensor(SCM[:], SCQ[:], SCR[:], ALU.subtract)
                        r.on("vector", elu1_fn, inc="dve")
                        r.on("scalar", lambda s: s.activation(SCM[:], SCM[:], AF.Exp),
                             inc="act")
                        r.on("vector", lambda v, dst=dst, half=half: v.tensor_tensor(
                            dst[:, half * 512:(half + 1) * 512], SCR[:], SCM[:],
                            ALU.add), inc="dve")
                    else:
                        r.on("vector", lambda v, dst=dst, half=half: v.tensor_scalar_mul(
                            dst[:, half * 512:(half + 1) * 512], SCQ[:], 1.0), inc="dve")

            Q3 = QTok[:].rearrange("p (h d) -> p h d", d=64)
            K3 = KTok[:].rearrange("p (h d) -> p h d", d=64)
            V3 = VTok[:].rearrange("p (h d) -> p h d", d=64)
            P3 = PRD[:].rearrange("p (h d) -> p h d", d=64)
            A3 = AA[:].rearrange("p (h g) -> p h g", g=16)
            for h in range(16):
                def amul_fn(v, h=h):
                    v.tensor_tensor(P3, K3, Q3[:, h:h + 1, :].broadcast_to([128, 16, 64]),
                                    ALU.mult)
                    return v.tensor_reduce(A3[:, h:h + 1, :].transpose([0, 2, 1])
                                           if False else
                                           AA[:, h * 16:(h + 1) * 16]
                                           .rearrange("p (g o) -> p g o", o=1),
                                           P3, AX, ALU.add)
                r.on("vector", amul_fn, inc="dve")
            r.on("vector", lambda v: v.tensor_reduce(
                DN[:, 0:16].rearrange("p (h o) -> p h o", o=1),
                AA[:].rearrange("p (h g) -> p h g", g=16), AX, ALU.add), inc="dve")
            r.on("vector", lambda v: v.reciprocal(DN[:, 16:32], DN[:, 0:16]), inc="dve")
            for g in range(16):
                def oacc_fn(v, g=g):
                    src = AA[:].rearrange("p (h g) -> p h g", g=16)[:, :, g:g + 1] \
                        .broadcast_to([128, 16, 64])
                    vv = V3[:, g:g + 1, :].broadcast_to([128, 16, 64])
                    o3 = OACC[:].rearrange("p (h d) -> p h d", d=64)
                    if g == 0:
                        return v.tensor_tensor(o3, src, vv, ALU.mult)
                    v.tensor_tensor(P3, src, vv, ALU.mult)
                    return v.tensor_tensor(OACC[:], OACC[:], PRD[:], ALU.add)
                r.on("vector", oacc_fn, inc="dve")
            r.on("vector", lambda v: v.tensor_tensor(
                ATok[:].rearrange("p (h d) -> p h d", d=64),
                OACC[:].rearrange("p (h d) -> p h d", d=64),
                DN[:, 16:32].rearrange("p (h o) -> p h o", o=1)
                .broadcast_to([128, 16, 64]), ALU.mult), inc="dve")

            for c8 in range(8):
                r.on("tensor", lambda t, c8=c8: t.transpose(
                    ps_tr[:], ATok[:, c8 * 128:(c8 + 1) * 128], IDB[:]), inc="pe")
                r.on("scalar", lambda s, c8=c8: s.copy(
                    AFM[:, c8 * 128:(c8 + 1) * 128], ps_tr[:]), inc="act")
            for f in range(8):
                r.dma("gpsimd", OW[:].rearrange("p (c w) -> p c w", w=128),
                      wo_i[:, f * 128:(f + 1) * 128].rearrange("(c p) w -> p c w", p=128))

                def opj_fn(t, f=f):
                    for c in range(8):
                        li = t.matmul(ps_a[0:128, 0:128],
                                      OW[:, c * 128:(c + 1) * 128],
                                      AFM[:, c * 128:(c + 1) * 128],
                                      start=(c == 0), stop=(c == 7))
                    return li
                r.on("tensor", opj_fn, inc="pe")
                r.on("vector", lambda v, f=f, tb=tb: v.scalar_tensor_tensor(
                    X[:, f * T + tb:f * T + tb + 128], ps_a[0:128, 0:128],
                    OBP[:, f:f + 1], X[:, f * T + tb:f * T + tb + 128],
                    ALU.add, ALU.add), inc="dve")

        for tt_ in range(4):
            tb = tt_ * 512

            def sl_in(c, j0, w, tb=tb):
                return X[:, c * T + tb + j0: c * T + tb + j0 + w]

            def sl_out(c, j0, w):
                return ZT[:, c * 512 + j0: c * 512 + j0 + w]
            rec_ln(r, kit, tiles, sl_in, sl_out, 512, gb=(LN2G, LN2B))

            for oc in range(32):
                r.dma("gpsimd", WAR[:, 0:8 * 128].rearrange("p (c w) -> p c w", w=128),
                      fc_i[:, oc * 128:(oc + 1) * 128].rearrange("(c p) w -> p c w", p=128))

                def fc_fn(t):
                    for c in range(8):
                        li = t.matmul(ps_a[:], WAR[:, c * 128:(c + 1) * 128],
                                      ZT[:, c * 512:(c + 1) * 512],
                                      start=(c == 0), stop=(c == 7))
                    return li
                r.on("tensor", fc_fn, inc="pe")
                r.on("scalar", lambda s, oc=oc: s.activation(
                    GT[:, oc * 512:(oc + 1) * 512], ps_a[:], AF.Gelu,
                    bias=FCB[:, oc:oc + 1]), inc="act")

            for f in range(8):
                r.dma("gpsimd", WAR[:, 0:32 * 128].rearrange("p (c w) -> p c w", w=128),
                      pj_i[:, f * 128:(f + 1) * 128].rearrange("(c p) w -> p c w", p=128))

                def pj_fn(t):
                    for c in range(32):
                        li = t.matmul(ps_a[:], WAR[:, c * 128:(c + 1) * 128],
                                      GT[:, c * 512:(c + 1) * 512],
                                      start=(c == 0), stop=(c == 31))
                    return li
                r.on("tensor", pj_fn, inc="pe")
                r.on("vector", lambda v, f=f, tb=tb: v.scalar_tensor_tensor(
                    X[:, f * T + tb:f * T + tb + 512], ps_a[:], PJB[:, f:f + 1],
                    X[:, f * T + tb:f * T + tb + 512], ALU.add, ALU.add), inc="dve")

        r.on("gpsimd", lambda g: g.dma_start(
            out=X[:].rearrange("p (c w) -> p c w", w=T),
            in_=x_i[:].rearrange("(c p) w -> p c w", p=128),
            accum_op=ALU.add), inc="wdma", inc_by=16)
        r.dma("sync", xo[:].rearrange("(c p) w -> p c w", p=128),
              X[:].rearrange("p (c w) -> p c w", w=T), inc="st")
        r.on("sync", lambda e: e.nop(), waits=r.snap())
        r.finalize(block)
    return nc


def l1attn_inputs(x_fm, c, W):
    m = {
        "x": _slice_own(x_fm, c),
        "ln1g": _pp(W["L1_attn_ln1_w"]),
        "ln1b": _pp(W["L1_attn_ln1_b"]),
        "ln2g": _pp(W["L1_attn_ln2_w"]),
        "ln2b": _pp(W["L1_attn_ln2_b"]),
        "wq": W["L1_attn_q_w_t"],
        "wk": W["L1_attn_k_w_t"],
        "wv": W["L1_attn_v_w_t"],
        "wo": W["L1_attn_o_w_t"],
        "qb": np.asarray(W["L1_attn_q_b"], np.float32).reshape(1, D),
        "kb": np.asarray(W["L1_attn_k_b"], np.float32).reshape(1, D),
        "vb": np.asarray(W["L1_attn_v_b"], np.float32).reshape(1, D),
        "ob": _pp(W["L1_attn_o_b"]),
        "fc": W["L1_attn_fc_w_t"],
        "fcb": _pp(W["L1_attn_fc_b"], 32),
        "pj": W["L1_attn_proj_w_t"],
        "pjb": _pp(W["L1_attn_proj_b"]),
        "c_ident": consts()["ident"],
    }
    m.update(const_in_map(["ones128", "ones1x128"]))
    return m


# version 02
# speedup vs baseline: 1.0106x; 1.0106x over previous
import numpy as np
import ml_dtypes
import concourse.bass as bass
import concourse.mybir as mybir
from concourse.bass_utils import run_bass_kernel_spmd

F32 = mybir.dt.float32
BF16 = mybir.dt.bfloat16
AF = mybir.ActivationFunctionType
ALU = mybir.AluOpType
AX = mybir.AxisListType.X

NCORES = 8
B = 2
SEQ = 8192
D = 1024
T = 2048
HALO = 513
NEG = -1.0e30
BF = ml_dtypes.bfloat16

ENGINES = ("sync", "gpsimd", "tensor", "vector", "scalar")



def _host_consts():
    c = {}
    c["ones128"] = np.ones((128, 1), np.float32)
    c["ones1x128"] = np.ones((1, 128), np.float32)
    h2 = np.zeros((128, 2), np.float32)
    h2[0:64, 0] = 1.0
    h2[64:128, 1] = 1.0
    c["head2"] = h2
    h2e = np.zeros((2, 128), np.float32)
    h2e[0, 0:64] = 1.0
    h2e[1, 64:128] = 1.0
    c["head2exp"] = h2e
    c["ident"] = np.eye(128, dtype=np.float32)
    c["onesrow"] = np.ones((1, SEQ), np.float32)
    kp = np.zeros((4, SEQ), np.float32)
    kp[0, :] = 1.0
    c["kpre"] = kp
    qp = np.zeros((3, SEQ), np.float32)
    qp[0:2, :] = 1.0
    c["qpre"] = qp
    c["jrow"] = np.arange(SEQ, dtype=np.float32)[None, :]
    c["j16"] = np.tile(np.arange(SEQ, dtype=np.float32)[None, :], (16, 1))
    ht_ = np.arange(16) % 4
    c["ebits"] = (ht_ & 1).astype(np.float32)[:, None]

    mL = np.zeros((8, 128, 512), np.float32)
    for m in range(8):
        r = np.arange(128)[:, None]
        cc = np.arange(512)[None, :]
        diff = cc + 512 - 128 * m - r
        mL[m] = np.where((diff >= 0) & (diff <= 512), 0.0, NEG)
    c["maskL"] = np.ascontiguousarray(mL.transpose(1, 0, 2).reshape(128, 8 * 512))
    c["maskL0"] = np.ascontiguousarray(
        np.full((128, 4 * 512), NEG, np.float32))

    mC = np.zeros((5, 128, 512), np.float32)
    for rel in range(4):
        r = np.arange(128)[:, None]
        cc = np.arange(512)[None, :]
        mC[rel] = np.where(rel * 128 + r <= cc, 0.0, NEG)
    mC[4] = NEG
    c["maskC"] = np.ascontiguousarray(mC.transpose(1, 0, 2).reshape(128, 5 * 512))

    pos = np.arange(SEQ, dtype=np.float64)
    inv = 10000.0 ** (-(np.arange(0, 64, 2, dtype=np.float64) / 64))
    th = pos[None, :] * inv[:, None]
    cosA = np.cos(th).astype(np.float32)
    sinA = np.sin(th).astype(np.float32)
    c["att_t1"] = np.concatenate([cosA, cosA], 0)
    c["att_t2"] = np.concatenate([-sinA, sinA], 0)
    c["att_t1q"] = c["att_t1"] * 0.125
    c["att_t2q"] = c["att_t2"] * 0.125

    DH = 64
    WIN = 512
    half = np.arange(0, DH, 2, dtype=np.float64)
    invf = 1.0 / 10000.0 ** (half / DH)
    base = (half + 0.4 * DH) / (1.4 * DH)

    def tabs(tglob, expo):
        fr = tglob[None, :] * invf[:, None]
        sc = base[:, None] ** (expo[None, :] / 256.0)
        cosF = np.cos(fr) * sc
        sinF = np.sin(fr) * sc
        t1 = np.concatenate([cosF, cosF], 0)
        t2 = np.concatenate([-sinF, sinF], 0)
        return t1.astype(np.float32), t2.astype(np.float32)

    tmod = np.arange(SEQ, dtype=np.float64) % WIN
    tg = np.arange(SEQ, dtype=np.float64)
    c["ffq_t1"], c["ffq_t2"] = tabs(tg, tmod)
    c["ffka_t1"], c["ffka_t2"] = tabs(tg, -tmod)
    c["ffkb_t1"], c["ffkb_t2"] = tabs(tg, (WIN - tmod))
    c["ffq_t1"] = c["ffq_t1"] * 8.0
    c["ffq_t2"] = c["ffq_t2"] * 8.0
    return c


_CONSTS = None


def consts():
    global _CONSTS
    if _CONSTS is None:
        _CONSTS = _host_consts()
    return _CONSTS


def core_pos(c):
    return c // 4, (c % 4) * T


def _pp(vec, nchunk=8):
    v = np.asarray(vec, np.float32).reshape(nchunk, 128)
    return np.ascontiguousarray(v.T)


def _pp_pad(vec, nchunk, size):
    v = np.zeros(nchunk * 128, np.float32)
    v[:size] = np.asarray(vec, np.float32).ravel()[:size]
    return _pp(v, nchunk)


def _fm(x_btd):
    return np.ascontiguousarray(np.transpose(np.asarray(x_btd, np.float32), (0, 2, 1)))


def _slice_own(x_fm, c):
    b, t0 = core_pos(c)
    return np.ascontiguousarray(x_fm[b, :, t0:t0 + T])


def _slice_halo(x_fm, c, n=HALO):
    b, t0 = core_pos(c)
    out = np.zeros((D, n), np.float32)
    lo = t0 - n
    if lo >= 0:
        out[:] = x_fm[b, :, lo:t0]
    elif t0 > 0:
        out[:, n - t0:] = x_fm[b, :, 0:t0]
    return out


def _assemble_fm(outs):
    full = np.empty((B, D, SEQ), np.float32)
    for c in range(NCORES):
        b, t0 = core_pos(c)
        full[b, :, t0:t0 + T] = outs[c]
    return full



class Rec:
    def __init__(self, nc):
        self.nc = nc
        self.ops = {e: [] for e in ENGINES}
        self.s = {}
        self.c = {}

    def sem(self, n):
        if n not in self.s:
            self.s[n] = self.nc.alloc_semaphore(name="sm_" + n)
            self.c[n] = 0
        return self.s[n]

    def snap(self):
        return dict(self.c)

    def on(self, engine, fn, inc=None, inc_by=1, waits=None):
        if waits is None:
            waits = self.snap()
        waits = {n: v for n, v in waits.items() if v > 0}
        for n in waits:
            self.sem(n)
        if inc is not None:
            self.sem(inc)
            self.c[inc] += inc_by
        self.ops[engine].append((fn, waits, inc, inc_by))

    def dma(self, queue, out, in_, inc=None, slow=False):
        if inc is None:
            inc = {"sync": "dma", "gpsimd": "wdma", "scalar": "adma"}[queue]
        self.on(queue, lambda e, o=out, i=in_: e.dma_start(
            out=o, in_=i, allow_slow_non_contiguous=slow), inc=inc, inc_by=16)

    def finalize(self, block):
        for eng_name in ENGINES:
            ops = self.ops[eng_name]
            if not ops:
                continue

            def body(eng, ops=ops):
                last_waits = {}
                for fn, waits, inc, inc_by in ops:
                    for n, v in waits.items():
                        if last_waits.get(n, -1) < v:
                            eng.wait_ge(self.s[n], v)
                            last_waits[n] = v
                    li = fn(eng)
                    if inc is not None:
                        assert li is not None, f"op for inc={inc} returned None"
                        li.then_inc(self.s[inc], inc_by)
            getattr(block, eng_name)(body)


def _new_nc():
    return bass.Bass("TRN2", target_bir_lowering=False, debug=False)


def _inp(ctx, nc, name, shape, dtype=F32):
    if ctx is not None:
        return ctx["aps"][name]
    return _din(nc, name, shape, dtype)


def _outp(ctx, nc, name, shape, dtype=F32):
    if ctx is not None:
        return ctx["aps"][name]
    return _dout(nc, name, shape, dtype)


def _ctx_open(ctx, pfx):
    if ctx is None:
        nc = _new_nc()
        return nc, Rec(nc), Scope(nc, pfx), True
    return ctx["nc"], ctx["r"], Scope(ctx["nc"], pfx), False


class Scope:

    def __init__(self, nc, pfx=""):
        self.nc = nc
        self.pfx = pfx
        self.gs = []

    def sb(self, name, shape, dtype=F32):
        g = self.nc.sbuf_tensor(self.pfx + name, list(shape), dtype)
        t = g.__enter__()
        self.gs.append(g)
        return t

    def ps(self, name, shape, dtype=F32):
        g = self.nc.psum_tensor(self.pfx + name, list(shape), dtype)
        t = g.__enter__()
        self.gs.append(g)
        return t

    def close(self):
        for g in reversed(self.gs):
            g.__exit__(None, None, None)


def _din(nc, name, shape, dtype=F32):
    return nc.dram_tensor(name, list(shape), dtype, kind="ExternalInput").ap()


def _dout(nc, name, shape, dtype=F32):
    return nc.dram_tensor(name, list(shape), dtype, kind="ExternalOutput").ap()


def _sb(nc, name, shape, dtype=F32):
    return nc.alloc_sbuf_tensor(name, list(shape), dtype)


def _psum(nc, name, shape, dtype=F32):
    return nc.alloc_psum_tensor(name, list(shape), dtype)


def load_consts(r, nc, names):
    cst = consts()
    tiles = {}
    aps = {}
    for n in names:
        arr = cst[n]
        aps[n] = _din(nc, "c_" + n, arr.shape, F32)
        tiles[n] = _sb(nc, "t_" + n, arr.shape, F32)
        r.dma("sync", tiles[n][:], aps[n][:])
    return tiles, aps


def load_consts_bf(r, nc, names):
    cst = consts()
    tiles = {}
    aps = {}
    for n in names:
        arr = cst[n]
        aps[n] = _din(nc, "c_" + n, arr.shape, F32)
        tiles[n] = _sb(nc, "b_" + n, arr.shape, BF16)
        r.dma("gpsimd", tiles[n][:], aps[n][:])
    return tiles, aps


def const_in_map(names):
    cst = consts()
    return {"c_" + n: cst[n] for n in names}



class LNKit:
    def __init__(self, nc):
        self.sq = _sb(nc, "ln_sq", [128, 3 * 512], F32)
        self.st = _sb(nc, "ln_st", [1, 4 * 512], F32)
        self.nsc = _sb(nc, "ln_nsc", [2, 2 * 512], F32)
        self.ps_s0 = _psum(nc, "ln_ps0", [2, 512], F32)
        self.ps_s1 = _psum(nc, "ln_ps1", [1, 512], F32)
        self.ps_b0 = _psum(nc, "ln_pb0", [128, 512], F32)
        self.ps_b1 = _psum(nc, "ln_pb1", [128, 512], F32)


def rec_ln(r, kit, tiles, sl_in, sl_out, width, gb=None):
    nt = (width + 511) // 512
    for it in range(nt):
        j0 = it * 512
        w = min(512, width - j0)

        def sums_fn(t, j0=j0, w=w):
            for c in range(8):
                li = t.matmul(kit.ps_s0[0:1, 0:w], tiles["ones128"][:],
                              sl_in(c, j0, w), start=(c == 0), stop=(c == 7))
            return li
        r.on("tensor", sums_fn, inc="pe")
        for c in range(8):
            r.on("scalar", lambda s, c=c, j0=j0, w=w: s.activation(
                kit.sq[:, 0:w], sl_in(c, j0, w), AF.Square), inc="act")
            r.on("tensor", lambda t, c=c, w=w: t.matmul(
                kit.ps_s1[0:1, 0:w], tiles["ones128"][:], kit.sq[:, 0:w],
                start=(c == 0), stop=(c == 7)), inc="pe")

        MU, WRK, RSTD, NMR = (kit.st[0:1, i * 512:i * 512 + 512] for i in range(4))

        def st1_fn(v, w=w):
            v.tensor_scalar_mul(MU[:, 0:w], kit.ps_s0[0:1, 0:w], 1.0 / D)
            v.tensor_scalar_mul(WRK[:, 0:w], kit.ps_s1[0:1, 0:w], 1.0 / D)
            v.tensor_tensor(RSTD[:, 0:w], MU[:, 0:w], MU[:, 0:w], ALU.mult)
            v.tensor_tensor(WRK[:, 0:w], WRK[:, 0:w], RSTD[:, 0:w], ALU.subtract)
            return v.tensor_scalar_add(WRK[:, 0:w], WRK[:, 0:w], 1e-5)
        r.on("vector", st1_fn, inc="dve")

        r.on("scalar", lambda s, w=w: s.activation(NMR[:, 0:w], WRK[:, 0:w],
                                                   AF.Sqrt), inc="act")

        def st2_fn(v, w=w):
            v.reciprocal(RSTD[:, 0:w], NMR[:, 0:w])
            return v.scalar_tensor_tensor(NMR[:, 0:w], MU[:, 0:w], -1.0,
                                          RSTD[:, 0:w], ALU.mult, ALU.mult)
        r.on("vector", st2_fn, inc="dve")

        def bc_fn(t, w=w):
            t.matmul(kit.ps_b0[:, 0:w], tiles["ones1x128"][:], RSTD[:, 0:w],
                     start=True, stop=True)
            return t.matmul(kit.ps_b1[:, 0:w], tiles["ones1x128"][:], NMR[:, 0:w],
                            start=True, stop=True)
        r.on("tensor", bc_fn, inc="pe")

        def ap_fn(v, j0=j0, w=w):
            for c in range(8):
                src = sl_in(c, j0, w)
                dst = sl_out(c, j0, w)
                v.tensor_tensor(kit.sq[:, 0:w], src, kit.ps_b0[:, 0:w], ALU.mult)
                if gb is None:
                    li = v.tensor_tensor(dst, kit.sq[:, 0:w],
                                         kit.ps_b1[:, 0:w], ALU.add)
                else:
                    v.tensor_tensor(kit.sq[:, 0:w], kit.sq[:, 0:w],
                                    kit.ps_b1[:, 0:w], ALU.add)
                    li = v.tensor_scalar(dst, kit.sq[:, 0:w],
                                         gb[0][:, c:c + 1], gb[1][:, c:c + 1],
                                         ALU.mult, ALU.add)
            return li
        r.on("vector", ap_fn, inc="dve")


def rec_ln_stream(r, kit, tiles, stream, chnk, width, out=None, out_chnk=None, gb=None):
    def sl_in(c, j0, w):
        return stream[:, c * chnk + j0:c * chnk + j0 + w]
    if out is None:
        sl_out = sl_in
    else:
        def sl_out(c, j0, w):
            return out[:, c * out_chnk + j0:c * out_chnk + j0 + w]
    rec_ln(r, kit, tiles, sl_in, sl_out, width, gb=gb)



def rec_norm_rope(r, tiles, kit, ps_in, ps_n, ps_bc, scale_pp, t1, t2, tj0, w, dsts):
    sc1 = kit.sq[:, 0:512]
    sc2 = kit.sq[:, 512:1024]
    sc3 = kit.sq[:, 1024:1536]
    nrm = kit.nsc[0:2, 0:512]
    nri = kit.nsc[0:2, 512:1024]

    r.on("scalar", lambda s: s.activation(sc1[:, 0:w], ps_in[:, 0:w], AF.Square),
         inc="act")
    r.on("tensor", lambda t: t.matmul(ps_n[0:2, 0:w], tiles["head2"][:], sc1[:, 0:w],
                                      start=True, stop=True), inc="pe")
    r.on("scalar", lambda s: s.activation(nrm[:, 0:w], ps_n[0:2, 0:w], AF.Sqrt),
         inc="act")

    def clip_fn(v):
        v.tensor_scalar_max(nrm[:, 0:w], nrm[:, 0:w], 1e-12)
        return v.reciprocal(nri[:, 0:w], nrm[:, 0:w])
    r.on("vector", clip_fn, inc="dve")

    r.on("tensor", lambda t: t.matmul(ps_bc[:, 0:w], tiles["head2exp"][:],
                                      nri[:, 0:w], start=True, stop=True), inc="pe")
    r.on("scalar", lambda s: s.copy(sc1[:, 0:w], ps_in[:, 0:w]), inc="act")

    def fin_fn(v):
        v.scalar_tensor_tensor(sc2[:, 0:w], sc1[:, 0:w], scale_pp[:, 0:1],
                               ps_bc[:, 0:w], ALU.mult, ALU.mult)
        li = None
        for sub in range(2):
            b0 = 64 * sub
            dst = dsts[sub]
            v.tensor_scalar_mul(sc1[0:64, 0:w], sc2[b0:b0 + 64, 0:w], 1.0)
            v.tensor_scalar_mul(sc3[0:32, 0:w], sc2[b0 + 32:b0 + 64, 0:w], 1.0)
            v.tensor_scalar_mul(sc3[32:64, 0:w], sc2[b0:b0 + 32, 0:w], 1.0)
            v.tensor_tensor(sc1[0:64, 0:w], sc1[0:64, 0:w],
                            t1[0:64, tj0:tj0 + w], ALU.mult)
            v.tensor_tensor(sc3[0:64, 0:w], sc3[0:64, 0:w],
                            t2[0:64, tj0:tj0 + w], ALU.mult)
            li = v.tensor_tensor(dst[0:64, 0:w], sc1[0:64, 0:w], sc3[0:64, 0:w], ALU.add)
        return li
    r.on("vector", fin_fn, inc="dve")


def rec_rope_plain(r, kit, ps_in, t1, t2, tj0, w, dsts, do_rope, scl):
    sc1 = kit.sq[:, 0:512]
    sc2 = kit.sq[:, 512:1024]
    sc3 = kit.sq[:, 1024:1536]
    if not do_rope:
        def cp_fn(s):
            s.activation(dsts[0], ps_in[0:64, 0:w], AF.Copy, scale=scl)
            return s.activation(dsts[1], ps_in[64:128, 0:w], AF.Copy, scale=scl)
        r.on("scalar", cp_fn, inc="act")
        return
    r.on("scalar", lambda s: s.copy(sc1[:, 0:w], ps_in[:, 0:w]), inc="act")

    def fin_fn(v):
        li = None
        for sub in range(2):
            b0 = 64 * sub
            dst = dsts[sub]
            v.tensor_scalar_mul(sc2[0:64, 0:w], sc1[b0:b0 + 64, 0:w], 1.0)
            v.tensor_scalar_mul(sc3[0:32, 0:w], sc1[b0 + 32:b0 + 64, 0:w], 1.0)
            v.tensor_scalar_mul(sc3[32:64, 0:w], sc1[b0:b0 + 32, 0:w], 1.0)
            v.tensor_tensor(sc2[0:64, 0:w], sc2[0:64, 0:w],
                            t1[0:64, tj0:tj0 + w], ALU.mult)
            v.tensor_tensor(sc3[0:64, 0:w], sc3[0:64, 0:w],
                            t2[0:64, tj0:tj0 + w], ALU.mult)
            li = v.tensor_tensor(dst[0:64, 0:w], sc2[0:64, 0:w], sc3[0:64, 0:w], ALU.add)
        return li
    r.on("vector", fin_fn, inc="dve")



def rec_attn_blocks(r, kt, kdim, qt, vt, pt, ps_S, ps_O, qj0, qw, kblocks,
                    add_masks, vt_stride=65):
    n = len(kblocks)
    for i, kb in enumerate(kblocks):
        r.on("tensor", lambda t, kb=kb: t.matmul(
            ps_S[:, 0:qw], kt[0:kdim, kb * 128:(kb + 1) * 128],
            qt[0:kdim, qj0:qj0 + qw], start=True, stop=True), inc="pe")
        if add_masks is not None and add_masks[i] is not None:
            r.on("vector", lambda v, m=add_masks[i]: v.tensor_tensor(
                ps_S[:, 0:qw], ps_S[:, 0:qw], m, ALU.add), inc="dve")
        r.on("scalar", lambda s: s.activation(pt[:, 0:qw], ps_S[:, 0:qw], AF.Exp),
             inc="act")
        r.on("tensor", lambda t, kb=kb, i=i: t.matmul(
            ps_O[:, 0:qw], vt[:, kb * vt_stride:kb * vt_stride + 65],
            pt[:, 0:qw], start=(i == 0), stop=(i == n - 1)), inc="pe")


def rec_o_norm(r, tiles, kit, ps_O, ps_bc, osb, rcp, dst, w):
    r.on("vector", lambda v: v.reciprocal(rcp[0:1, 0:w], ps_O[64:65, 0:w]), inc="dve")
    r.on("scalar", lambda s: s.copy(osb[0:64, 0:w], ps_O[0:64, 0:w]), inc="act")
    r.on("tensor", lambda t: t.matmul(ps_bc[0:64, 0:w], tiles["ones1x128"][0:1, 0:64],
                                      rcp[0:1, 0:w], start=True, stop=True), inc="pe")

    def div_fn(v):
        v.tensor_tensor(kit.sq[0:64, 0:w], osb[0:64, 0:w], ps_bc[0:64, 0:w], ALU.mult)
        return v.tensor_scalar_mul(dst, kit.sq[0:64, 0:w], 1.0)
    r.on("vector", div_fn, inc="dve")



def rec_attn_win(r, kt, kdim, qt, vt, pt, ps_S, ps_O, qw, blocks):
    n = len(blocks)
    for i, (kbk, kbv, msk) in enumerate(blocks):
        r.on("tensor", lambda t, kbk=kbk: t.matmul(
            ps_S[:, 0:qw], kt[0:kdim, kbk * 128:(kbk + 1) * 128],
            qt[0:kdim, 0:qw], start=True, stop=True), inc="pe")
        if msk is not None:
            r.on("vector", lambda v, m=msk: v.tensor_tensor(
                ps_S[:, 0:qw], ps_S[:, 0:qw], m, ALU.add), inc="dve")
        r.on("scalar", lambda s: s.activation(pt[:, 0:qw], ps_S[:, 0:qw], AF.Exp),
             inc="act")
        r.on("tensor", lambda t, kbv=kbv, i=i: t.matmul(
            ps_O[:, 0:qw], vt[:, kbv * 65:kbv * 65 + 65],
            pt[:, 0:qw], start=(i == 0), stop=(i == n - 1)), inc="pe")



def build_ffnA(pre_ln, ctx=None, pfx="", halo_fill=None):
    nc, r, sc, ext = _ctx_open(ctx, pfx)
    KW = 2560
    HBW = 2561

    src = _inp(ctx, nc, "src", (D, T))
    srch = _inp(ctx, nc, "srch", (D, HALO)) if halo_fill is None else None
    wqkv_i = _inp(ctx, nc, "wqkv", (D, 1536))
    wo_i = _inp(ctx, nc, "wo", (512, D))
    qs_i = _inp(ctx, nc, "qs", (128, 1))
    ks_i = _inp(ctx, nc, "ks", (128, 1))
    mW0_i = _inp(ctx, nc, "mW0", (128, 4 * 512))
    ffq1_i = _inp(ctx, nc, "ffq1", (64, T))
    ffq2_i = _inp(ctx, nc, "ffq2", (64, T))
    ffka1_i = _inp(ctx, nc, "ffka1", (64, KW))
    ffka2_i = _inp(ctx, nc, "ffka2", (64, KW))
    ffkb1_i = _inp(ctx, nc, "ffkb1", (64, KW))
    ffkb2_i = _inp(ctx, nc, "ffkb2", (64, KW))
    umid_o = _outp(ctx, nc, "umid", (D, T))

    U = sc.sb("U", [128, 8 * T], F32)
    HB = sc.sb("HBs", [128, 8 * HBW], BF16)
    WQ = sc.sb("WQs", [128, 8 * 384], BF16)
    WO = sc.sb("WOs", [128, 4 * 128], BF16)
    QS = sc.sb("QSs", [128, 1], F32)
    KS = sc.sb("KSs", [128, 1], F32)
    MW0 = sc.sb("MW0", [128, 4 * 512], BF16)
    MKL = sc.sb("MKL", [128, 8 * 512], BF16)
    FQ1 = sc.sb("FQ1", [64, T], BF16)
    FQ2 = sc.sb("FQ2", [64, T], BF16)
    FKA1 = sc.sb("FKA1", [64, KW], BF16)
    FKA2 = sc.sb("FKA2", [64, KW], BF16)
    FKB1 = sc.sb("FKB1", [64, KW], BF16)
    FKB2 = sc.sb("FKB2", [64, KW], BF16)
    kit = ctx["kit"] if ctx else LNKit(nc)
    tiles, caps = (ctx["tiles"], ctx["caps"]) if ctx else load_consts(r, nc, ["ones128", "ones1x128", "head2",
                                      "head2exp"])
    cst = consts()
    kpre_ap = _inp(ctx, nc, "c_kpre", cst["kpre"].shape, F32)
    ident_ap = _inp(ctx, nc, "c_ident", cst["ident"].shape, F32)
    maskL_ap = _inp(ctx, nc, "c_maskL", cst["maskL"].shape, F32)

    uh_guard = nc.sbuf_tensor(pfx + "UH", [128, 8 * HALO], F32)
    UH = uh_guard.__enter__()

    r.dma("sync", U[:].rearrange("p (c w) -> p c w", w=T),
          src[:].rearrange("(c p) w -> p c w", p=128))
    if halo_fill is None:
        r.dma("sync", UH[:].rearrange("p (c w) -> p c w", w=HALO),
              srch[:].rearrange("(c p) w -> p c w", p=128))
    else:
        halo_fill(UH)
    r.dma("sync", QS[:], qs_i[:])
    r.dma("sync", KS[:], ks_i[:])
    r.dma("gpsimd", MW0[:], mW0_i[:])
    r.dma("gpsimd", MKL[:], maskL_ap[:])
    r.dma("gpsimd", FQ1[:], ffq1_i[:])
    r.dma("gpsimd", FQ2[:], ffq2_i[:])
    r.dma("gpsimd", FKA1[:], ffka1_i[:])
    r.dma("gpsimd", FKA2[:], ffka2_i[:])
    r.dma("gpsimd", FKB1[:], ffkb1_i[:])
    r.dma("gpsimd", FKB2[:], ffkb2_i[:])

    if pre_ln:
        rec_ln_stream(r, kit, tiles, U, T, T)
        rec_ln_stream(r, kit, tiles, UH, HALO, HALO)

    def hb_fn(v):
        for c in range(8):
            v.tensor_copy(HB[:, c * HBW:c * HBW + HALO], UH[:, c * HALO:(c + 1) * HALO])
            li = v.tensor_copy(HB[:, c * HBW + HALO:(c + 1) * HBW], U[:, c * T:(c + 1) * T])
        return li
    r.on("vector", hb_fn, inc="dve")

    uh_guard.__exit__(None, None, None)
    IDB = sc.sb("IDB", [128, 128], BF16)
    KTw = [sc.sb(f"KTw{i}", [65, 1024], BF16) for i in range(2)]
    QTw = [sc.sb(f"QTw{i}", [65, 512], BF16) for i in range(2)]
    VT = [sc.sb(f"VT{i}", [128, 20 * 65], BF16) for i in range(2)]
    OALL = [sc.sb(f"OALL{i}", [128, T], BF16) for i in range(4)]
    PT = sc.sb("PTs", [128, 512], BF16)
    MXC = sc.sb("MXC", [128, 4], F32)
    MXB = sc.sb("MXB", [128, 4], BF16)
    MT = sc.sb("MTs", [4, 128], BF16)
    OSB = sc.sb("OSB", [64, 512], F32)
    RCP = sc.sb("RCPs", [1, 512], F32)

    ps_qkv = kit.ps_b0
    ps_n = kit.ps_s0
    ps_bc = kit.ps_b1
    ps_S = sc.ps("ps_S", [128, 640], F32)
    ps_O = sc.ps("ps_O", [65, 512], F32)
    ps_T = sc.ps("ps_T", [4, 128], BF16)

    r.dma("gpsimd", IDB[:], ident_ap[:])
    for i in range(2):
        r.dma("gpsimd", KTw[i][64:65, :], kpre_ap[0:1, 0:1024])
        r.on("gpsimd", lambda g, i=i: g.memset(
            VT[i][:].rearrange("p (b c) -> p b c", c=65)[:, :, 64:65], 1.0),
            inc="pool")

    def hslice(c, j0, w):
        off = 1 if c < 4 else 0
        return HB[:, c * HBW + j0 + off: c * HBW + j0 + off + w]

    for hp in range(4):
        r.dma("gpsimd", WQ[:].rearrange("p (c w) -> p c w", w=384)[:, :, 0:128],
              wqkv_i[:, hp * 128:(hp + 1) * 128].rearrange("(c p) w -> p c w", p=128))
        r.dma("gpsimd", WQ[:].rearrange("p (c w) -> p c w", w=384)[:, :, 128:256],
              wqkv_i[:, 512 + hp * 128:512 + (hp + 1) * 128].rearrange("(c p) w -> p c w", p=128))
        r.dma("gpsimd", WQ[:].rearrange("p (c w) -> p c w", w=384)[:, :, 256:384],
              wqkv_i[:, 1024 + hp * 128:1024 + (hp + 1) * 128].rearrange("(c p) w -> p c w", p=128))

        for blk in range(20):
            j0 = blk * 128

            def vmm_fn(t, j0=j0):
                for sub in range(2):
                    for c in range(8):
                        li = t.matmul(ps_qkv[0:128, sub * 64:sub * 64 + 64],
                                      hslice(c, j0, 128),
                                      WQ[:, c * 384 + 256 + sub * 64:c * 384 + 256 + sub * 64 + 64],
                                      start=(c == 0), stop=(c == 7))
                return li
            r.on("tensor", vmm_fn, inc="pe")

            def vev_fn(s, blk=blk):
                s.copy(VT[0][:, blk * 65:blk * 65 + 64], ps_qkv[0:128, 0:64])
                return s.copy(VT[1][:, blk * 65:blk * 65 + 64], ps_qkv[0:128, 64:128])
            r.on("scalar", vev_fn, inc="act")

        for w in range(4):
            w0 = w * 512
            for half in range(2):
                kj0 = w0 + half * 512

                def kmm_fn(t, kj0=kj0):
                    for c in range(8):
                        li = t.matmul(ps_qkv[:], WQ[:, c * 384 + 128:c * 384 + 256],
                                      hslice(c, kj0, 512), start=(c == 0), stop=(c == 7))
                    return li
                r.on("tensor", kmm_fn, inc="pe")
                t1 = FKA1 if half == 1 else FKB1
                t2 = FKA2 if half == 1 else FKB2
                rec_norm_rope(r, tiles, kit, ps_qkv, ps_n, ps_bc, KS,
                              t1, t2, kj0, 512,
                              (KTw[0][0:64, half * 512:(half + 1) * 512],
                               KTw[1][0:64, half * 512:(half + 1) * 512]))

            def qmm_fn(t, w0=w0):
                for c in range(8):
                    li = t.matmul(ps_qkv[:], WQ[:, c * 384:c * 384 + 128],
                                  hslice(c, 512 + w0, 512), start=(c == 0), stop=(c == 7))
                return li
            r.on("tensor", qmm_fn, inc="pe")
            rec_norm_rope(r, tiles, kit, ps_qkv, ps_n, ps_bc, QS,
                          FQ1, FQ2, w0, 512,
                          (QTw[0][0:64, :], QTw[1][0:64, :]))

            for sub in range(2):
                Kt, Qt, Vt = KTw[sub], QTw[sub], VT[sub]
                for qs in range(4):
                    def mx_fn(t, qs=qs, Kt=Kt, Qt=Qt):
                        t.matmul(ps_S[:, 0:512], Qt[0:64, qs * 128:(qs + 1) * 128],
                                 Kt[0:64, qs * 128:qs * 128 + 512], start=True, stop=True)
                        return t.matmul(ps_S[:, 512:640],
                                        Qt[0:64, qs * 128:(qs + 1) * 128],
                                        Kt[0:64, qs * 128 + 512:qs * 128 + 640],
                                        start=True, stop=True)
                    r.on("tensor", mx_fn, inc="pe")
                    r.on("vector", lambda v, qs=qs: v.reduce_max(
                        MXC[:, qs:qs + 1], ps_S[:, 0:640], AX), inc="dve")
                r.on("vector", lambda v: v.tensor_scalar_mul(MXB[:], MXC[:], -1.0),
                     inc="dve")
                r.on("tensor", lambda t: t.transpose(ps_T[:], MXB[:], IDB[:]),
                     inc="pe")
                r.on("scalar", lambda s: s.copy(MT[:], ps_T[:]), inc="act")
                for qs in range(4):
                    r.dma("scalar", Qt[64:65, qs * 128:(qs + 1) * 128],
                          MT[qs:qs + 1, :])
                blocks = []
                for m in range(8):
                    if w == 0 and m < 4:
                        msk = MW0[:, m * 512:(m + 1) * 512]
                    else:
                        msk = MKL[:, m * 512:(m + 1) * 512]
                    blocks.append((m, 4 * w + m, msk))
                rec_attn_win(r, Kt, 65, Qt, Vt, PT, ps_S, ps_O, 512, blocks)
                hh = 2 * hp + sub
                dst = OALL[hh // 2][(hh % 2) * 64:(hh % 2) * 64 + 64, w0:w0 + 512]
                rec_o_norm(r, tiles, kit, ps_O, ps_bc, OSB, RCP, dst, 512)

    for f in range(8):
        r.dma("gpsimd", WO[:].rearrange("p (c w) -> p c w", w=128),
              wo_i[:, f * 128:(f + 1) * 128].rearrange("(c p) w -> p c w", p=128))
        for tt_ in range(4):
            j0 = tt_ * 512

            def wo_fn(t, f=f, j0=j0):
                for c in range(4):
                    li = t.matmul(ps_qkv[:], WO[:, c * 128:(c + 1) * 128],
                                  OALL[c][:, j0:j0 + 512], start=(c == 0), stop=(c == 3))
                return li
            r.on("tensor", wo_fn, inc="pe")
            r.on("vector", lambda v, f=f, j0=j0: v.tensor_tensor(
                U[:, f * T + j0:f * T + j0 + 512], U[:, f * T + j0:f * T + j0 + 512],
                ps_qkv[:], ALU.add), inc="dve")

    r.dma("sync", umid_o[:].rearrange("(c p) w -> p c w", p=128),
          U[:].rearrange("p (c w) -> p c w", w=T), inc="st")
    r.on("sync", lambda e: e.nop(), waits=r.snap())
    if not ext:
        sc.close()
    if ext:
        with nc.Block() as block:
            r.finalize(block)
    return nc


def ffnA_inputs(stream_fm, c, layer, W, pre_ln):
    cst = consts()
    b, t0 = core_pos(c)

    def ksl(name):
        out = np.zeros((64, 2560), np.float32)
        lo = t0 - 512
        s = max(lo, 0)
        out[:, s - lo:] = cst[name][:, s:t0 + T]
        return out

    m = {
        "src": _slice_own(stream_fm, c),
        "srch": _slice_halo(stream_fm, c),
        "wqkv": np.ascontiguousarray(W["L0_ffn_wqkv"][layer]),
        "wo": np.ascontiguousarray(W["L0_ffn_wo"][layer]),
        "qs": np.tile(np.asarray(W["L0_ffn_q_scale"][layer], np.float32).reshape(64, 1), (2, 1)),
        "ks": np.tile(np.asarray(W["L0_ffn_k_scale"][layer], np.float32).reshape(64, 1), (2, 1)),
        "mW0": cst["maskL0"] if t0 == 0 else np.ascontiguousarray(cst["maskL"][:, 0:4 * 512]),
        "ffq1": np.ascontiguousarray(cst["ffq_t1"][:, t0:t0 + T]),
        "ffq2": np.ascontiguousarray(cst["ffq_t2"][:, t0:t0 + T]),
        "ffka1": ksl("ffka_t1"),
        "ffka2": ksl("ffka_t2"),
        "ffkb1": ksl("ffkb_t1"),
        "ffkb2": ksl("ffkb_t2"),
    }
    m.update(const_in_map(["ones128", "ones1x128", "head2", "head2exp",
                           "kpre", "ident", "maskL"]))
    return m



def build_ffnB(final_x, ctx=None, pfx="", prev_fill=None):
    nc, r, sc, ext = _ctx_open(ctx, pfx)
    UW = T + 1

    u_i = _inp(ctx, nc, "u", (D, T))
    up_i = _inp(ctx, nc, "uprev", (D, 1)) if prev_fill is None else None
    lng_i = _inp(ctx, nc, "lng", (128, 8))
    lnb_i = _inp(ctx, nc, "lnb", (128, 8))
    w1_i = _inp(ctx, nc, "w1", (D, 5460))
    b1a_i = _inp(ctx, nc, "b1a", (128, 22))
    b1g_i = _inp(ctx, nc, "b1g", (128, 22))
    w2_i = _inp(ctx, nc, "w2", (2730, D))
    b2_i = _inp(ctx, nc, "b2", (128, 8))
    if final_x:
        x_i = _inp(ctx, nc, "x", (D, T))
        out_o = _outp(ctx, nc, "xout", (D, T))
    else:
        out_o = _outp(ctx, nc, "uout", (D, T))

    U = sc.sb("U", [128, 8 * UW], F32)
    Z = sc.sb("Z", [128, 8 * 512], BF16)
    HT = sc.sb("HT", [128, 22 * 512], BF16)
    W1C = sc.sb("W1C", [128, 8 * 256], BF16)
    W2C = sc.sb("W2C", [128, 22 * 128], BF16)
    GEL = sc.sb("GEL", [128, 512], BF16)
    LNG = sc.sb("LNG", [128, 8], F32)
    LNB = sc.sb("LNB", [128, 8], F32)
    B1A = sc.sb("B1A", [128, 22], F32)
    B1G = sc.sb("B1G", [128, 22], F32)
    B2 = sc.sb("B2s", [128, 8], F32)
    if final_x:
        X = sc.sb("X", [128, 8 * T], F32)

    ps_a = sc.ps("ps_a", [128, 512], F32)
    ps_g = sc.ps("ps_g", [128, 512], F32)
    ps_w2 = sc.ps("ps_w2", [128, 512], F32)

    kit = ctx["kit"] if ctx else LNKit(nc)
    tiles, caps = (ctx["tiles"], ctx["caps"]) if ctx else load_consts(r, nc, ["ones128", "ones1x128"])

    r.dma("sync", U[:].rearrange("p (c w) -> p c w", w=UW)[:, :, 1:UW],
          u_i[:].rearrange("(c p) w -> p c w", p=128))
    if prev_fill is None:
        r.dma("sync", U[:].rearrange("p (c w) -> p c w", w=UW)[:, :, 0:1],
              up_i[:].rearrange("(c p) w -> p c w", p=128), slow=True)
    else:
        prev_fill(U)
    r.dma("sync", LNG[:], lng_i[:])
    r.dma("sync", LNB[:], lnb_i[:])
    r.dma("sync", B1A[:], b1a_i[:])
    r.dma("sync", B1G[:], b1g_i[:])
    r.dma("sync", B2[:], b2_i[:])
    if final_x:
        r.dma("sync", X[:].rearrange("p (c w) -> p c w", w=T),
              x_i[:].rearrange("(c p) w -> p c w", p=128))

    for tt_ in reversed(range(4)):
        tb = tt_ * 512

        def sl_in(c, j0, w, tb=tb):
            off = 1 if c < 4 else 0
            return U[:, c * UW + off + tb + j0: c * UW + off + tb + j0 + w]

        def sl_out(c, j0, w):
            return Z[:, c * 512 + j0: c * 512 + j0 + w]

        rec_ln(r, kit, tiles, sl_in, sl_out, 512, gb=(LNG, LNB))

        for jc in range(22):
            acw = 128 if jc < 21 else 42
            r.dma("gpsimd", W1C[:].rearrange("p (c w) -> p c w", w=256)[:, :, 0:acw],
                  w1_i[:, jc * 128:jc * 128 + acw].rearrange("(c p) w -> p c w", p=128))
            r.dma("gpsimd", W1C[:].rearrange("p (c w) -> p c w", w=256)[:, :, 128:128 + acw],
                  w1_i[:, 2730 + jc * 128:2730 + jc * 128 + acw].rearrange("(c p) w -> p c w", p=128))

            def ag_fn(t, jc=jc, acw=acw):
                for c in range(8):
                    t.matmul(ps_a[0:acw, :], W1C[:, c * 256:c * 256 + acw],
                             Z[:, c * 512:(c + 1) * 512],
                             start=(c == 0), stop=(c == 7))
                for c in range(8):
                    li = t.matmul(ps_g[0:acw, :], W1C[:, c * 256 + 128:c * 256 + 128 + acw],
                                  Z[:, c * 512:(c + 1) * 512],
                                  start=(c == 0), stop=(c == 7))
                return li
            r.on("tensor", ag_fn, inc="pe")
            r.on("scalar", lambda s, jc=jc, acw=acw: s.activation(
                GEL[0:acw, :], ps_g[0:acw, :], AF.Gelu, bias=B1G[0:acw, jc:jc + 1]),
                inc="act")
            r.on("vector", lambda v, jc=jc, acw=acw: v.scalar_tensor_tensor(
                HT[0:acw, jc * 512:(jc + 1) * 512], ps_a[0:acw, :],
                B1A[0:acw, jc:jc + 1], GEL[0:acw, :], ALU.add, ALU.mult),
                inc="dve")

        for f in range(8):
            r.dma("gpsimd", W2C[:, 0:21 * 128],
                  w2_i[0:21 * 128, f * 128:(f + 1) * 128].rearrange("(c p) w -> p c w", p=128))
            r.dma("gpsimd", W2C[0:42, 21 * 128:22 * 128],
                  w2_i[21 * 128:2730, f * 128:(f + 1) * 128])

            def w2_fn(t, f=f):
                for c in range(22):
                    kc = 128 if c < 21 else 42
                    li = t.matmul(ps_w2[:], W2C[0:kc, c * 128:(c + 1) * 128],
                                  HT[0:kc, c * 512:(c + 1) * 512],
                                  start=(c == 0), stop=(c == 21))
                return li
            r.on("tensor", w2_fn, inc="pe")
            r.on("vector", lambda v, f=f, tb=tb: v.scalar_tensor_tensor(
                U[:, f * UW + 1 + tb:f * UW + 1 + tb + 512], ps_w2[:],
                B2[:, f:f + 1], U[:, f * UW + 1 + tb:f * UW + 1 + tb + 512],
                ALU.add, ALU.add), inc="dve")

    if final_x:
        def xres_fn(v):
            for f in range(8):
                li = v.scalar_tensor_tensor(
                    X[:, f * T:(f + 1) * T], U[:, f * UW + 1:f * UW + 1 + T],
                    0.5, X[:, f * T:(f + 1) * T], ALU.mult, ALU.add)
            return li
        r.on("vector", xres_fn, inc="dve")
        r.dma("sync", out_o[:].rearrange("(c p) w -> p c w", p=128),
              X[:].rearrange("p (c w) -> p c w", w=T), inc="st")
    else:
        r.dma("sync", out_o[:].rearrange("(c p) w -> p c w", p=128),
              U[:].rearrange("p (c w) -> p c w", w=UW)[:, :, 1:UW], inc="st")
    r.on("sync", lambda e: e.nop(), waits=r.snap())
    if not ext:
        sc.close()
    if ext:
        with nc.Block() as block:
            r.finalize(block)
    return nc


def ffnB_inputs(u_fm, c, layer, W, x_fm=None):
    m = {
        "u": _slice_own(u_fm, c),
        "uprev": _slice_halo(u_fm, c, 1),
        "lng": _pp(W["L0_ffn_ln_g"][layer]),
        "lnb": _pp(W["L0_ffn_ln_b"][layer]),
        "w1": np.ascontiguousarray(W["L0_ffn_w1"][layer]),
        "b1a": _pp_pad(W["L0_ffn_b1"][layer][:2730], 22, 2730),
        "b1g": _pp_pad(W["L0_ffn_b1"][layer][2730:], 22, 2730),
        "w2": np.ascontiguousarray(W["L0_ffn_w2"][layer]),
        "b2": _pp(W["L0_ffn_b2"][layer]),
    }
    if x_fm is not None:
        m["x"] = _slice_own(x_fm, c)
    m.update(const_in_map(["ones128", "ones1x128"]))
    return m



def prep_weights(inputs):
    W = {}
    for k, v in inputs.items():
        W[k] = np.asarray(v, np.float32)
    wq = W["L0_attn_w_qkv"].copy()
    ht = np.arange(16) % 4
    perm64 = np.concatenate([np.arange(0, 64, 2), np.arange(1, 64, 2)])
    for h in range(16):
        if (ht[h] >> 1) & 1:
            for base in (0, 1024):
                cols = base + h * 64 + np.arange(64)
                wq[:, cols] = wq[:, cols[perm64]]
    W["att_wqkv_perm"] = np.ascontiguousarray(wq)
    for k in ["L1_attn_q_w", "L1_attn_k_w", "L1_attn_v_w", "L1_attn_o_w",
              "L1_attn_fc_w", "L1_attn_proj_w", "L1_ffn_w1_w", "L1_ffn_w2_w",
              "L1_ffn_h1_w", "L1_ffn_h2_w", "L1_ffn_w3_w"]:
        W[k + "_t"] = np.ascontiguousarray(W[k].T)
    return W


_NC_CACHE = {}


def _nc_get(key, fn):
    if key not in _NC_CACHE:
        _NC_CACHE[key] = fn()
    return _NC_CACHE[key]


def _run(nc, in_maps):
    return run_bass_kernel_spmd(nc, in_maps, core_ids=list(range(NCORES)))


def _ffn_phase(x_fm, W):
    ncA0 = _nc_get("A0", lambda: build_ffnA(pre_ln=True))
    ncA1 = _nc_get("A1", lambda: build_ffnA(pre_ln=False))
    ncB0 = _nc_get("B0", lambda: build_ffnB(final_x=False))
    ncB1 = _nc_get("B1", lambda: build_ffnB(final_x=True))
    res = _run(ncA0, [ffnA_inputs(x_fm, c, 0, W, True) for c in range(NCORES)])
    m_fm = _assemble_fm([res.results[c]["umid"] for c in range(NCORES)])
    res = _run(ncB0, [ffnB_inputs(m_fm, c, 0, W) for c in range(NCORES)])
    u1_fm = _assemble_fm([res.results[c]["uout"] for c in range(NCORES)])
    res = _run(ncA1, [ffnA_inputs(u1_fm, c, 1, W, False) for c in range(NCORES)])
    m2_fm = _assemble_fm([res.results[c]["umid"] for c in range(NCORES)])
    res = _run(ncB1, [ffnB_inputs(m2_fm, c, 1, W, x_fm=x_fm) for c in range(NCORES)])
    return _assemble_fm([res.results[c]["xout"] for c in range(NCORES)])


def _attn_phase(x_fm, W):
    nc1 = _nc_get("Q", build_attn_qkv)
    nc2 = _nc_get("C", build_attn_core)
    nc3 = _nc_get("P", build_attn_proj)
    res = _run(nc1, [attn_qkv_inputs(x_fm, c, W) for c in range(NCORES)])
    qf = np.empty((B, D, SEQ), BF)
    kf = np.empty((B, D, SEQ), BF)
    vf = np.empty((B, SEQ, D), BF)
    for c in range(NCORES):
        b, t0 = core_pos(c)
        qf[b, :, t0:t0 + T] = res.results[c]["q_o"]
        kf[b, :, t0:t0 + T] = res.results[c]["k_o"]
        vo = res.results[c]["v_o"]
        for h in range(16):
            vf[b, t0:t0 + T, h * 64:(h + 1) * 64] = vo[h * T:(h + 1) * T]
    chif = np.empty((16, SEQ), BF)
    clof = np.empty((16, SEQ), BF)
    for c in range(4):
        b, t0 = core_pos(c)
        chif[:, t0:t0 + T] = res.results[c]["chi"]
        clof[:, t0:t0 + T] = res.results[c]["clo"]
    res2 = _run(nc2, [attn_core_inputs(qf, kf, vf, chif, clof, c, W)
                      for c in range(NCORES)])
    o_fm = np.empty((B, D, SEQ), BF)
    for c in range(NCORES):
        b = c // 4
        h0 = (c % 4) * 4
        o_fm[b, h0 * 64:(h0 + 4) * 64, :] = res2.results[c]["o4"]
    res3 = _run(nc3, [attn_proj_inputs(x_fm, o_fm, c, W) for c in range(NCORES)])
    return _assemble_fm([res3.results[c]["xout"] for c in range(NCORES)])


def kernel(**inputs):
    W = prep_weights(inputs)
    x_fm = _fm(inputs["L0_attn_x"])
    x_fm = _ffn_phase(x_fm, W)
    x_fm = _attn_phase(x_fm, W)
    x_fm = _ffn_phase(x_fm, W)
    ncF = _nc_get("F", build_l1ffn)
    ncL = _nc_get("L", build_l1attn)
    res = _run(ncF, [l1ffn_inputs(x_fm, c, W) for c in range(NCORES)])
    x_fm = _assemble_fm([res.results[c]["xout"] for c in range(NCORES)])
    res = _run(ncL, [l1attn_inputs(x_fm, c, W) for c in range(NCORES)])
    x_fm = _assemble_fm([res.results[c]["xout"] for c in range(NCORES)])
    res = _run(ncF, [l1ffn_inputs(x_fm, c, W) for c in range(NCORES)])
    x_fm = _assemble_fm([res.results[c]["xout"] for c in range(NCORES)])
    return np.ascontiguousarray(np.transpose(x_fm, (0, 2, 1)))



def build_attn_qkv(ctx=None, pfx=""):
    nc, r, sc, ext = _ctx_open(ctx, pfx)

    x_i = _inp(ctx, nc, "x", (D, T))
    wq_i = _inp(ctx, nc, "wqkv", (D, 3072))
    lx_i = _inp(ctx, nc, "logxi", (16, 1))
    eb_i = _inp(ctx, nc, "ebits", (16, 1))
    j16_i = _inp(ctx, nc, "j16", (16, T))
    aq1_i = _inp(ctx, nc, "aq1", (64, T))
    aq2_i = _inp(ctx, nc, "aq2", (64, T))
    ak1_i = _inp(ctx, nc, "ak1", (64, T))
    ak2_i = _inp(ctx, nc, "ak2", (64, T))
    q_o = _outp(ctx, nc, "q_o", (D, T), BF16)
    k_o = _outp(ctx, nc, "k_o", (D, T), BF16)
    v_o = _outp(ctx, nc, "v_o", (16 * T, 64), BF16)
    chi_o = _outp(ctx, nc, "chi", (16, T), BF16)
    clo_o = _outp(ctx, nc, "clo", (16, T), BF16)

    X = sc.sb("X", [128, 8 * T], F32)
    Z = sc.sb("Z", [128, 8 * T], BF16)
    WC = sc.sb("WC", [128, 8 * 512], BF16)
    STG = sc.sb("STG", [128, 512], BF16)
    AQ1 = sc.sb("AQ1", [64, T], BF16)
    AQ2 = sc.sb("AQ2", [64, T], BF16)
    AK1 = sc.sb("AK1", [64, T], BF16)
    AK2 = sc.sb("AK2", [64, T], BF16)
    LX = sc.sb("LX", [16, 1], F32)
    EB = sc.sb("EB", [16, 1], F32)
    J16 = sc.sb("J16", [16, T], F32)
    C32 = sc.sb("C32", [16, 2 * T], F32)
    CBF = sc.sb("CBF", [16, 2 * T], BF16)

    ht = np.arange(16) % 4
    PI = [(int(h) >> 1) & 1 for h in ht]

    kit = ctx["kit"] if ctx else LNKit(nc)
    tiles, caps = (ctx["tiles"], ctx["caps"]) if ctx else load_consts(r, nc, ["ones128", "ones1x128"])
    ps_mm = kit.ps_b0
    ps_n = kit.ps_s0

    r.dma("sync", X[:].rearrange("p (c w) -> p c w", w=T),
          x_i[:].rearrange("(c p) w -> p c w", p=128))
    r.dma("gpsimd", AQ1[:], aq1_i[:])
    r.dma("gpsimd", AQ2[:], aq2_i[:])
    r.dma("gpsimd", AK1[:], ak1_i[:])
    r.dma("gpsimd", AK2[:], ak2_i[:])
    r.dma("sync", LX[:], lx_i[:])
    r.dma("sync", EB[:], eb_i[:])
    r.dma("sync", J16[:], j16_i[:])

    rec_ln_stream(r, kit, tiles, X, T, T, out=Z, out_chnk=T)

    r.on("scalar", lambda s: s.activation(C32[0:16, 0:1], LX[:], AF.Exp,
                                          scale=-1.0), inc="act")
    r.on("vector", lambda v: v.tensor_tensor(C32[0:16, 1:2], C32[0:16, 0:1],
                                             EB[:], ALU.mult), inc="dve")

    def crow_fn(v):
        v.tensor_scalar_mul(C32[0:16, T:2 * T], J16[:], C32[0:16, 1:2])
        v.tensor_scalar_mul(CBF[0:16, 0:T], C32[0:16, T:2 * T], 1.0)
        v.tensor_scalar_mul(C32[0:16, 0:T], CBF[0:16, 0:T], 1.0)
        v.tensor_tensor(C32[0:16, 0:T], C32[0:16, T:2 * T], C32[0:16, 0:T],
                        ALU.subtract)
        return v.tensor_scalar_mul(CBF[0:16, T:2 * T], C32[0:16, 0:T], 1.0)
    r.on("vector", crow_fn, inc="dve")
    r.dma("sync", chi_o[:], CBF[0:16, 0:T], inc="st")
    r.dma("sync", clo_o[:], CBF[0:16, T:2 * T], inc="st")

    for role in range(2):
        base = role * 1024
        t1 = AQ1 if role == 0 else AK1
        t2 = AQ2 if role == 0 else AK2
        out_ap = q_o if role == 0 else k_o
        for oc in range(8):
            r.dma("gpsimd", WC[:, 0:8 * 128].rearrange("p (c w) -> p c w", w=128),
                  wq_i[:, base + oc * 128:base + (oc + 1) * 128]
                  .rearrange("(c p) w -> p c w", p=128))
            for it in range(4):
                j0 = it * 512

                def mm_fn(t, j0=j0):
                    for c in range(8):
                        li = t.matmul(ps_mm[:], WC[:, c * 128:(c + 1) * 128],
                                      Z[:, c * T + j0:c * T + j0 + 512],
                                      start=(c == 0), stop=(c == 7))
                    return li
                r.on("tensor", mm_fn, inc="pe")
                h0, h1 = 2 * oc, 2 * oc + 1
                scl = 0.125 if role == 0 else 1.0
                if PI[h0]:
                    rec_rope_plain(r, kit, ps_mm, t1, t2, j0, 512,
                                   (STG[0:64, :], STG[64:128, :]), True, scl)
                else:
                    rec_rope_plain(r, kit, ps_mm, t1, t2, j0, 512,
                                   (STG[0:64, :], STG[64:128, :]), False, scl)
                r.dma("sync", out_ap[oc * 128:(oc + 1) * 128, j0:j0 + 512],
                      STG[:], inc="st")


    for vh in range(2):
        r.dma("gpsimd", WC[:].rearrange("p (c w) -> p c w", w=512),
              wq_i[:, 2048 + vh * 512:2048 + (vh + 1) * 512]
              .rearrange("(c p) w -> p c w", p=128))
        for tc in range(16):
            def vmm_fn(t, tc=tc):
                for c in range(8):
                    li = t.matmul(ps_mm[:], Z[:, c * T + tc * 128:c * T + (tc + 1) * 128],
                                  WC[:, c * 512:(c + 1) * 512],
                                  start=(c == 0), stop=(c == 7))
                return li
            r.on("tensor", vmm_fn, inc="pe")
            r.on("scalar", lambda s: s.copy(STG[:], ps_mm[:]), inc="act")
            for hh in range(8):
                h = vh * 8 + hh
                r.dma("sync", v_o[h * T + tc * 128:h * T + (tc + 1) * 128, :],
                      STG[:, hh * 64:(hh + 1) * 64], inc="st")

    r.on("sync", lambda e: e.nop(), waits=r.snap())
    if not ext:
        sc.close()
    if ext:
        with nc.Block() as block:
            r.finalize(block)
    return nc


def attn_qkv_inputs(x_fm, c, W):
    cst = consts()
    b, t0 = core_pos(c)
    m = {
        "x": _slice_own(x_fm, c),
        "wqkv": W["att_wqkv_perm"],
        "logxi": np.asarray(W["L0_attn_log_xi"], np.float32).reshape(16, 1),
        "ebits": cst["ebits"],
        "j16": np.ascontiguousarray(cst["j16"][:, t0:t0 + T]),
        "aq1": np.ascontiguousarray(cst["att_t1q"][:, t0:t0 + T]),
        "aq2": np.ascontiguousarray(cst["att_t2q"][:, t0:t0 + T]),
        "ak1": np.ascontiguousarray(cst["att_t1"][:, t0:t0 + T]),
        "ak2": np.ascontiguousarray(cst["att_t2"][:, t0:t0 + T]),
    }
    m.update(const_in_map(["ones128", "ones1x128"]))
    return m



def build_attn_core(ctx=None, pfx="", fills=None):
    nc, r, sc, ext = _ctx_open(ctx, pfx)

    if fills is None:
        q4_i = _inp(ctx, nc, "q4", (256, SEQ), BF16)
        k4_i = _inp(ctx, nc, "k4", (256, SEQ), BF16)
        v4_i = _inp(ctx, nc, "v4", (4 * SEQ, 64), BF16)
        chi_i = _inp(ctx, nc, "chi4", (4, SEQ), BF16)
        clo_i = _inp(ctx, nc, "clo4", (4, SEQ), BF16)
    lxs_i = _inp(ctx, nc, "logxi4", (1, 4))
    ebs_i = _inp(ctx, nc, "ebits4", (1, 4))
    jr_i = _inp(ctx, nc, "jrow", (1, SEQ))
    o_o = _outp(ctx, nc, "o4", (256, SEQ), BF16)

    KT = sc.sb("KT", [68, SEQ], BF16)
    QT = sc.sb("QT", [68, SEQ], BF16)
    VT = sc.sb("VT", [128, 64 * 65], BF16)
    PT = sc.sb("PTs", [128, 1024], BF16)
    MC = sc.sb("MC", [128, 5 * 512], F32)
    JR = sc.sb("JR", [1, SEQ], F32)
    EXI = sc.sb("EXI", [1, 8], F32)
    SC = sc.sb("SCs", [64, 512], F32)
    ST = sc.sb("STs", [1, 4 * 512], F32)
    KMX = sc.sb("KMX", [1, 64], F32)
    OSB = sc.sb("OSB", [64, 1024], F32)
    RCP = sc.sb("RCPs", [1, 1024], F32)
    OST = sc.sb("OST", [64, 1024], BF16)

    ps_S = sc.ps("ps_S", [128, 1024], F32)
    ps_O = sc.ps("ps_O", [65, 1024], F32)
    ps_n = sc.ps("ps_n", [1, 512], F32)
    ps_bc = sc.ps("ps_bc", [64, 1024], F32)

    cst = consts()

    tiles, caps = (ctx["tiles"], ctx["caps"]) if ctx else load_consts(r, nc, ["ones128", "ones1x128"])
    kpre_ap = _inp(ctx, nc, "c_kpre", cst["kpre"].shape, F32)
    qpre_ap = _inp(ctx, nc, "c_qpre", cst["qpre"].shape, F32)
    maskC_ap = _inp(ctx, nc, "c_maskC", cst["maskC"].shape, F32)

    r.dma("sync", MC[:], maskC_ap[:])
    r.dma("sync", JR[:], jr_i[:])
    EXL = sc.sb("EXL", [1, 8], F32)
    r.dma("sync", EXL[0:1, 0:4], lxs_i[:])
    r.dma("sync", EXL[0:1, 4:8], ebs_i[:])
    r.on("scalar", lambda s: s.activation(EXI[0:1, 0:4], EXL[0:1, 0:4], AF.Exp,
                                          scale=-1.0), inc="act")
    r.on("vector", lambda v: v.tensor_tensor(EXI[0:1, 4:8], EXI[0:1, 0:4],
                                             EXL[0:1, 4:8], ALU.mult), inc="dve")
    r.on("gpsimd", lambda g: g.memset(
        VT[:].rearrange("p (b c) -> p b c", c=65)[:, :, 64:65], 1.0), inc="pool")

    for p_ in range(4):
        r.dma("gpsimd", KT[64:65, :], kpre_ap[0:1, :])
        r.dma("gpsimd", KT[67:68, :], kpre_ap[3:4, :])
        r.dma("gpsimd", QT[65:67, :], qpre_ap[0:2, :])
        r.dma("gpsimd", QT[67:68, :], qpre_ap[2:3, :])
        if fills is None:
            r.dma("sync", KT[0:64, :], k4_i[p_ * 64:(p_ + 1) * 64, :])
            r.dma("sync", KT[65:66, :], chi_i[p_:p_ + 1, :])
            r.dma("sync", KT[66:67, :], clo_i[p_:p_ + 1, :])
            r.dma("sync", QT[0:64, :], q4_i[p_ * 64:(p_ + 1) * 64, :])
            for kb in range(64):
                r.dma("sync", VT[:, kb * 65:kb * 65 + 64],
                      v4_i[p_ * SEQ + kb * 128:p_ * SEQ + (kb + 1) * 128, :])
        else:
            fills(p_, KT, QT, VT)

        for it in range(16):
            r.on("scalar", lambda s, it=it: s.activation(
                SC[:, 0:512], KT[0:64, it * 512:(it + 1) * 512], AF.Square),
                inc="act")
            r.on("tensor", lambda t: t.matmul(
                ps_n[0:1, 0:512], tiles["ones128"][0:64, :], SC[:, 0:512],
                start=True, stop=True), inc="pe")
            r.on("vector", lambda v, it=it: v.reduce_max(
                KMX[0:1, it:it + 1], ps_n[0:1, 0:512], AX), inc="dve")
        r.on("vector", lambda v: v.reduce_max(KMX[0:1, 32:33], KMX[0:1, 0:16], AX),
             inc="dve")
        r.on("scalar", lambda s: s.activation(KMX[0:1, 33:34], KMX[0:1, 32:33],
                                              AF.Sqrt), inc="act")

        for it in range(16):
            r.on("scalar", lambda s, it=it: s.activation(
                SC[:, 0:512], QT[0:64, it * 512:(it + 1) * 512], AF.Square),
                inc="act")
            r.on("tensor", lambda t: t.matmul(
                ps_n[0:1, 0:512], tiles["ones128"][0:64, :], SC[:, 0:512],
                start=True, stop=True), inc="pe")
            r.on("scalar", lambda s: s.activation(ST[0:1, 0:512],
                                                  ps_n[0:1, 0:512], AF.Sqrt),
                 inc="act")

            def brow_fn(v, it=it, p_=p_):
                v.tensor_scalar_mul(ST[0:1, 512:1024], ST[0:1, 0:512],
                                    KMX[0:1, 33:34])
                v.scalar_tensor_tensor(ST[0:1, 1024:1536],
                                       JR[0:1, it * 512:(it + 1) * 512],
                                       EXI[0:1, 4 + p_:5 + p_],
                                       ST[0:1, 512:1024], ALU.mult, ALU.add)
                return v.tensor_scalar_mul(QT[64:65, it * 512:(it + 1) * 512],
                                           ST[0:1, 1024:1536], -1.0)
            r.on("vector", brow_fn, inc="dve")

        for qt2 in range(8):
            q0 = qt2 * 1024
            nb = 8 * qt2 + 8
            for kb in range(nb):
                rel = kb - 8 * qt2

                def s_fn(t, kb=kb, q0=q0):
                    t.matmul(ps_S[:, 0:512], KT[0:68, kb * 128:(kb + 1) * 128],
                             QT[0:68, q0:q0 + 512], start=True, stop=True)
                    return t.matmul(ps_S[:, 512:1024],
                                    KT[0:68, kb * 128:(kb + 1) * 128],
                                    QT[0:68, q0 + 512:q0 + 1024],
                                    start=True, stop=True)
                r.on("tensor", s_fn, inc="pe")
                if rel >= 0:
                    def msk_fn(v, rel=rel):
                        li = None
                        if rel < 4:
                            li = v.tensor_tensor(ps_S[:, 0:512], ps_S[:, 0:512],
                                                 MC[:, rel * 512:(rel + 1) * 512],
                                                 ALU.add)
                        else:
                            v.tensor_tensor(ps_S[:, 0:512], ps_S[:, 0:512],
                                            MC[:, 4 * 512:5 * 512], ALU.add)
                            li = v.tensor_tensor(ps_S[:, 512:1024], ps_S[:, 512:1024],
                                                 MC[:, (rel - 4) * 512:(rel - 3) * 512],
                                                 ALU.add)
                        return li
                    r.on("vector", msk_fn, inc="dve")
                r.on("scalar", lambda s: s.activation(PT[:], ps_S[:], AF.Exp),
                     inc="act")

                def pv_fn(t, kb=kb, first=(kb == 0), last=(kb == nb - 1)):
                    t.matmul(ps_O[:, 0:512], VT[:, kb * 65:kb * 65 + 65],
                             PT[:, 0:512], start=first, stop=last)
                    return t.matmul(ps_O[:, 512:1024], VT[:, kb * 65:kb * 65 + 65],
                                    PT[:, 512:1024], start=first, stop=last)
                r.on("tensor", pv_fn, inc="pe")

            r.on("vector", lambda v: v.reciprocal(RCP[0:1, :], ps_O[64:65, :]),
                 inc="dve")
            r.on("scalar", lambda s: s.copy(OSB[:], ps_O[0:64, :]), inc="act")
            def bc2_fn(t):
                t.matmul(ps_bc[:, 0:512], tiles["ones1x128"][0:1, 0:64],
                         RCP[0:1, 0:512], start=True, stop=True)
                return t.matmul(ps_bc[:, 512:1024], tiles["ones1x128"][0:1, 0:64],
                                RCP[0:1, 512:1024], start=True, stop=True)
            r.on("tensor", bc2_fn, inc="pe")
            r.on("vector", lambda v: v.tensor_tensor(OST[:], OSB[:], ps_bc[:],
                                                     ALU.mult), inc="dve")
            r.dma("sync", o_o[p_ * 64:(p_ + 1) * 64, q0:q0 + 1024], OST[:],
                  inc="st")

    r.on("sync", lambda e: e.nop(), waits=r.snap())
    if not ext:
        sc.close()
    if ext:
        with nc.Block() as block:
            r.finalize(block)
    return nc


def attn_core_inputs(qf, kf, vf, chif, clof, c, W):
    cst = consts()
    b = c // 4
    h0 = (c % 4) * 4
    hs = list(range(h0, h0 + 4))
    q4 = np.concatenate([qf[b, h * 64:(h + 1) * 64, :] for h in hs], 0)
    k4 = np.concatenate([kf[b, h * 64:(h + 1) * 64, :] for h in hs], 0)
    v4 = np.concatenate([vf[b, :, h * 64:(h + 1) * 64] for h in hs], 0)
    ht = np.arange(16) % 4
    m = {
        "q4": np.ascontiguousarray(q4),
        "k4": np.ascontiguousarray(k4),
        "v4": np.ascontiguousarray(v4),
        "chi4": np.ascontiguousarray(chif[hs]),
        "clo4": np.ascontiguousarray(clof[hs]),
        "logxi4": np.asarray(W["L0_attn_log_xi"], np.float32).reshape(1, 16)[:, hs],
        "ebits4": np.ascontiguousarray(cst["ebits"].reshape(1, 16)[:, hs]),
        "jrow": cst["jrow"],
    }
    m.update(const_in_map(["ones128", "ones1x128", "kpre", "qpre", "maskC"]))
    return m



def build_attn_proj(ctx=None, pfx="", o_fill=None):
    nc, r, sc, ext = _ctx_open(ctx, pfx)

    x_i = _inp(ctx, nc, "x", (D, T))
    o_i = _inp(ctx, nc, "o", (D, T), BF16) if o_fill is None else None
    wo_i = _inp(ctx, nc, "wo", (D, D))
    xo = _outp(ctx, nc, "xout", (D, T))

    X = sc.sb("X", [128, 8 * T], F32)
    O = sc.sb("O", [128, 8 * T], BF16)
    WC = sc.sb("WC", [128, 8 * 128], BF16)
    ps = sc.ps("ps", [128, 512], F32)

    r.dma("sync", X[:].rearrange("p (c w) -> p c w", w=T),
          x_i[:].rearrange("(c p) w -> p c w", p=128))
    if o_fill is None:
        r.dma("sync", O[:].rearrange("p (c w) -> p c w", w=T),
              o_i[:].rearrange("(c p) w -> p c w", p=128))
    else:
        o_fill(O)
    for f in range(8):
        r.dma("gpsimd", WC[:].rearrange("p (c w) -> p c w", w=128),
              wo_i[:, f * 128:(f + 1) * 128].rearrange("(c p) w -> p c w", p=128))
        for it in range(4):
            j0 = it * 512

            def mm_fn(t, f=f, j0=j0):
                for c in range(8):
                    li = t.matmul(ps[:], WC[:, c * 128:(c + 1) * 128],
                                  O[:, c * T + j0:c * T + j0 + 512],
                                  start=(c == 0), stop=(c == 7))
                return li
            r.on("tensor", mm_fn, inc="pe")
            r.on("vector", lambda v, f=f, j0=j0: v.tensor_tensor(
                X[:, f * T + j0:f * T + j0 + 512],
                X[:, f * T + j0:f * T + j0 + 512], ps[:], ALU.add), inc="dve")

    r.dma("sync", xo[:].rearrange("(c p) w -> p c w", p=128),
          X[:].rearrange("p (c w) -> p c w", w=T), inc="st")
    r.on("sync", lambda e: e.nop(), waits=r.snap())
    if not ext:
        sc.close()
    if ext:
        with nc.Block() as block:
            r.finalize(block)
    return nc


def attn_proj_inputs(x_fm, o_fm, c, W):
    return {
        "x": _slice_own(x_fm, c),
        "o": _slice_own_bf(o_fm, c),
        "wo": W["L0_attn_w_o"],
    }


def _slice_own_bf(x_fm, c):
    b, t0 = core_pos(c)
    return np.ascontiguousarray(x_fm[b, :, t0:t0 + T])



def build_l1ffn(ctx=None, pfx=""):
    nc, r, sc, ext = _ctx_open(ctx, pfx)

    x_i = _inp(ctx, nc, "x", (D, T))
    w1_i = _inp(ctx, nc, "w1", (D, 4096))
    w2_i = _inp(ctx, nc, "w2", (D, 4096))
    h1_i = _inp(ctx, nc, "h1", (4096, 4096))
    h2_i = _inp(ctx, nc, "h2", (4096, 4096))
    w3_i = _inp(ctx, nc, "w3", (4096, D))
    s1_i = _inp(ctx, nc, "s1", (128, 32))
    s2_i = _inp(ctx, nc, "s2", (128, 32))
    sh1_i = _inp(ctx, nc, "sh1", (128, 32))
    sh2_i = _inp(ctx, nc, "sh2", (128, 32))
    s3_i = _inp(ctx, nc, "s3", (128, 8))
    xo = _outp(ctx, nc, "xout", (D, T))

    X = sc.sb("X", [128, 8 * T], F32)
    Z = sc.sb("Z", [128, 8 * 512], BF16)
    HA = sc.sb("HA", [128, 32 * 512], BF16)
    HBB = sc.sb("HB2", [128, 32 * 512], BF16)
    WCA = sc.sb("WCA", [128, 8 * 128], BF16)
    WCB = sc.sb("WCB", [128, 8 * 128], BF16)
    WH = sc.sb("WH", [128, 32 * 128], BF16)
    SA = sc.sb("SA", [128, 512], BF16)
    S1 = sc.sb("S1", [128, 32], F32)
    S2 = sc.sb("S2", [128, 32], F32)
    SH1 = sc.sb("SH1", [128, 32], F32)
    SH2 = sc.sb("SH2", [128, 32], F32)
    S3H = sc.sb("S3H", [128, 8], F32)

    ps_a = sc.ps("ps_a", [128, 512], F32)
    ps_g = sc.ps("ps_g", [128, 512], F32)

    kit = ctx["kit"] if ctx else LNKit(nc)
    tiles, caps = (ctx["tiles"], ctx["caps"]) if ctx else load_consts(r, nc, ["ones128", "ones1x128"])

    r.dma("sync", X[:].rearrange("p (c w) -> p c w", w=T),
          x_i[:].rearrange("(c p) w -> p c w", p=128))
    r.dma("sync", S1[:], s1_i[:])
    r.dma("sync", S2[:], s2_i[:])
    r.dma("sync", SH1[:], sh1_i[:])
    r.dma("sync", SH2[:], sh2_i[:])
    r.dma("sync", S3H[:], s3_i[:])
    r.on("vector", lambda v: v.tensor_scalar_mul(S3H[:], S3H[:], 0.5), inc="dve")

    for tt_ in range(4):
        tb = tt_ * 512

        def sl_in(c, j0, w, tb=tb):
            return X[:, c * T + tb + j0: c * T + tb + j0 + w]

        def sl_out(c, j0, w):
            return Z[:, c * 512 + j0: c * 512 + j0 + w]
        rec_ln(r, kit, tiles, sl_in, sl_out, 512)

        for oc in range(32):
            r.dma("gpsimd", WCA[:].rearrange("p (c w) -> p c w", w=128),
                  w1_i[:, oc * 128:(oc + 1) * 128].rearrange("(c p) w -> p c w", p=128))
            r.dma("gpsimd", WCB[:].rearrange("p (c w) -> p c w", w=128),
                  w2_i[:, oc * 128:(oc + 1) * 128].rearrange("(c p) w -> p c w", p=128))

            def mm1_fn(t):
                for c in range(8):
                    t.matmul(ps_a[:], WCA[:, c * 128:(c + 1) * 128],
                             Z[:, c * 512:(c + 1) * 512], start=(c == 0), stop=(c == 7))
                for c in range(8):
                    li = t.matmul(ps_g[:], WCB[:, c * 128:(c + 1) * 128],
                                  Z[:, c * 512:(c + 1) * 512], start=(c == 0), stop=(c == 7))
                return li
            r.on("tensor", mm1_fn, inc="pe")
            r.on("scalar", lambda s, oc=oc: s.activation(
                SA[:], ps_a[:], AF.Silu, scale=S1[:, oc:oc + 1]), inc="act")
            r.on("vector", lambda v, oc=oc: v.scalar_tensor_tensor(
                HA[:, oc * 512:(oc + 1) * 512], ps_g[:], S2[:, oc:oc + 1],
                SA[:], ALU.mult, ALU.mult), inc="dve")

        for oc in range(32):
            r.dma("gpsimd", WH[:].rearrange("p (c w) -> p c w", w=128),
                  h1_i[:, oc * 128:(oc + 1) * 128].rearrange("(c p) w -> p c w", p=128))

            def mm2_fn(t):
                for c in range(32):
                    li = t.matmul(ps_a[:], WH[:, c * 128:(c + 1) * 128],
                                  HA[:, c * 512:(c + 1) * 512],
                                  start=(c == 0), stop=(c == 31))
                return li
            r.on("tensor", mm2_fn, inc="pe")
            r.on("scalar", lambda s, oc=oc: s.activation(
                HBB[:, oc * 512:(oc + 1) * 512], ps_a[:], AF.Gelu,
                scale=SH1[:, oc:oc + 1]), inc="act")

        for oc in range(32):
            r.dma("gpsimd", WH[:].rearrange("p (c w) -> p c w", w=128),
                  h2_i[:, oc * 128:(oc + 1) * 128].rearrange("(c p) w -> p c w", p=128))

            def mm3_fn(t):
                for c in range(32):
                    li = t.matmul(ps_a[:], WH[:, c * 128:(c + 1) * 128],
                                  HBB[:, c * 512:(c + 1) * 512],
                                  start=(c == 0), stop=(c == 31))
                return li
            r.on("tensor", mm3_fn, inc="pe")
            r.on("scalar", lambda s, oc=oc: s.activation(
                HA[:, oc * 512:(oc + 1) * 512], ps_a[:], AF.Gelu,
                scale=SH2[:, oc:oc + 1]), inc="act")

        for f in range(8):
            r.dma("gpsimd", WH[:, 0:32 * 128].rearrange("p (c w) -> p c w", w=128),
                  w3_i[:, f * 128:(f + 1) * 128].rearrange("(c p) w -> p c w", p=128))

            def mm4_fn(t):
                for c in range(32):
                    li = t.matmul(ps_a[:], WH[:, c * 128:(c + 1) * 128],
                                  HA[:, c * 512:(c + 1) * 512],
                                  start=(c == 0), stop=(c == 31))
                return li
            r.on("tensor", mm4_fn, inc="pe")
            r.on("vector", lambda v, f=f, tb=tb: v.scalar_tensor_tensor(
                X[:, f * T + tb:f * T + tb + 512], ps_a[:], S3H[:, f:f + 1],
                X[:, f * T + tb:f * T + tb + 512], ALU.mult, ALU.add), inc="dve")

    r.dma("sync", xo[:].rearrange("(c p) w -> p c w", p=128),
          X[:].rearrange("p (c w) -> p c w", w=T), inc="st")
    r.on("sync", lambda e: e.nop(), waits=r.snap())
    if not ext:
        sc.close()
    if ext:
        with nc.Block() as block:
            r.finalize(block)
    return nc


def l1ffn_inputs(x_fm, c, W):
    m = {
        "x": _slice_own(x_fm, c),
        "w1": W["L1_ffn_w1_w_t"],
        "w2": W["L1_ffn_w2_w_t"],
        "h1": W["L1_ffn_h1_w_t"],
        "h2": W["L1_ffn_h2_w_t"],
        "w3": W["L1_ffn_w3_w_t"],
        "s1": _pp(W["L1_ffn_w1_s"], 32),
        "s2": _pp(W["L1_ffn_w2_s"], 32),
        "sh1": _pp(W["L1_ffn_h1_s"], 32),
        "sh2": _pp(W["L1_ffn_h2_s"], 32),
        "s3": _pp(W["L1_ffn_w3_s"], 8),
    }
    m.update(const_in_map(["ones128", "ones1x128"]))
    return m



def build_l1attn(ctx=None, pfx=""):
    nc, r, sc, ext = _ctx_open(ctx, pfx)

    x_i = _inp(ctx, nc, "x", (D, T))
    ln1g_i = _inp(ctx, nc, "ln1g", (128, 8))
    ln1b_i = _inp(ctx, nc, "ln1b", (128, 8))
    ln2g_i = _inp(ctx, nc, "ln2g", (128, 8))
    ln2b_i = _inp(ctx, nc, "ln2b", (128, 8))
    wq_i = _inp(ctx, nc, "wq", (D, D))
    wk_i = _inp(ctx, nc, "wk", (D, D))
    wv_i = _inp(ctx, nc, "wv", (D, D))
    wo_i = _inp(ctx, nc, "wo", (D, D))
    qb_i = _inp(ctx, nc, "qb", (1, D))
    kb_i = _inp(ctx, nc, "kb", (1, D))
    vb_i = _inp(ctx, nc, "vb", (1, D))
    ob_i = _inp(ctx, nc, "ob", (128, 8))
    fc_i = _inp(ctx, nc, "fc", (D, 4096))
    fcb_i = _inp(ctx, nc, "fcb", (128, 32))
    pj_i = _inp(ctx, nc, "pj", (4096, D))
    pjb_i = _inp(ctx, nc, "pjb", (128, 8))
    id_i = _inp(ctx, nc, "c_ident", (128, 128))
    xo = _outp(ctx, nc, "xout", (D, T))

    X = sc.sb("X", [128, 8 * T], F32)
    ZARENA = sc.sb("ZARENA", [128, 20 * 1024], BF16)
    Z = ZARENA[:, 0:8 * T]
    ZT = ZARENA[:, 0:8 * 512]
    GT = ZARENA[:, 4096:4096 + 32 * 512]
    WAR = sc.sb("WAR", [128, 24 * 1024], BF16)
    OW = sc.sb("OW", [128, 8 * 128], BF16)
    LN1G = sc.sb("LN1G", [128, 8], F32)
    LN1B = sc.sb("LN1B", [128, 8], F32)
    LN2G = sc.sb("LN2G", [128, 8], F32)
    LN2B = sc.sb("LN2B", [128, 8], F32)
    OBP = sc.sb("OBP", [128, 8], F32)
    FCB = sc.sb("FCB", [128, 32], F32)
    PJB = sc.sb("PJB", [128, 8], F32)
    BB = sc.sb("BB", [128, 3 * D], F32)
    QTok = sc.sb("QTok", [128, D], BF16)
    KTok = sc.sb("KTok", [128, D], BF16)
    VTok = sc.sb("VTok", [128, D], BF16)
    PRD = sc.sb("PRD", [128, D], F32)
    AA = sc.sb("AA", [128, 256], F32)
    DN = sc.sb("DN", [128, 32], F32)
    OACC = sc.sb("OACC", [128, D], F32)
    ATok = sc.sb("ATok", [128, D], BF16)
    AFM = sc.sb("AFM", [128, D], BF16)
    IDB = sc.sb("IDB", [128, 128], BF16)

    ps_a = sc.ps("ps_a", [128, 512], F32)
    ps_tr = sc.ps("ps_tr", [128, 128], BF16)

    kit = ctx["kit"] if ctx else LNKit(nc)
    tiles, caps = (ctx["tiles"], ctx["caps"]) if ctx else load_consts(r, nc, ["ones128", "ones1x128"])

    r.dma("sync", X[:].rearrange("p (c w) -> p c w", w=T),
          x_i[:].rearrange("(c p) w -> p c w", p=128))
    for tin, tsb in ((ln1g_i, LN1G), (ln1b_i, LN1B), (ln2g_i, LN2G),
                     (ln2b_i, LN2B), (ob_i, OBP), (fcb_i, FCB), (pjb_i, PJB)):
        r.dma("sync", tsb[:], tin[:])
    r.dma("sync", BB[:, 0:D], qb_i[:].broadcast_to([128, D]))
    r.dma("sync", BB[:, D:2 * D], kb_i[:].broadcast_to([128, D]))
    r.dma("sync", BB[:, 2 * D:3 * D], vb_i[:].broadcast_to([128, D]))
    r.dma("gpsimd", IDB[:], id_i[:])
    SCQ = kit.sq[:, 0:512]
    SCR = kit.sq[:, 512:1024]
    SCM = kit.sq[:, 1024:1536]

    rec_ln_stream(r, kit, tiles, X, T, T)

    def zaff_fn(v):
        for c in range(8):
            li = v.tensor_scalar(Z[:, c * T:(c + 1) * T], X[:, c * T:(c + 1) * T],
                                 LN1G[:, c:c + 1], LN1B[:, c:c + 1],
                                 ALU.mult, ALU.add)
        return li
    r.on("vector", zaff_fn, inc="dve")

    for wi, base in ((wq_i, 0), (wk_i, 8), (wv_i, 16)):
        r.dma("gpsimd", WAR[:, base * 1024:(base + 8) * 1024]
              .rearrange("p (c w) -> p c w", w=1024),
              wi[:].rearrange("(c p) w -> p c w", p=128))

    for tc in range(16):
        tb = tc * 128
        for pi_, (wbase, bseg, dst, do_elu) in enumerate(
                ((0, 0, QTok, True), (8, 1, KTok, True), (16, 2, VTok, False))):
            for half in range(2):
                def mm_fn(t, wbase=wbase, half=half, tb=tb):
                    for c in range(8):
                        li = t.matmul(ps_a[:],
                                      Z[:, c * T + tb:c * T + tb + 128],
                                      WAR[:, (wbase + c) * 1024 + half * 512:
                                          (wbase + c) * 1024 + (half + 1) * 512],
                                      start=(c == 0), stop=(c == 7))
                    return li
                r.on("tensor", mm_fn, inc="pe")

                def bias_fn(v, bseg=bseg, half=half):
                    return v.tensor_tensor(
                        SCQ[:], ps_a[:],
                        BB[:, bseg * 1024 + half * 512:bseg * 1024 + (half + 1) * 512],
                        ALU.add)
                r.on("vector", bias_fn, inc="dve")
                if do_elu:
                    def elu1_fn(v):
                        v.tensor_scalar_max(SCR[:], SCQ[:], 0.0)
                        return v.tensor_tensor(SCM[:], SCQ[:], SCR[:], ALU.subtract)
                    r.on("vector", elu1_fn, inc="dve")
                    r.on("scalar", lambda s: s.activation(SCM[:], SCM[:], AF.Exp),
                         inc="act")
                    r.on("vector", lambda v, dst=dst, half=half: v.tensor_tensor(
                        dst[:, half * 512:(half + 1) * 512], SCR[:], SCM[:],
                        ALU.add), inc="dve")
                else:
                    r.on("vector", lambda v, dst=dst, half=half: v.tensor_scalar_mul(
                        dst[:, half * 512:(half + 1) * 512], SCQ[:], 1.0), inc="dve")

        Q3 = QTok[:].rearrange("p (h d) -> p h d", d=64)
        K3 = KTok[:].rearrange("p (h d) -> p h d", d=64)
        V3 = VTok[:].rearrange("p (h d) -> p h d", d=64)
        P3 = PRD[:].rearrange("p (h d) -> p h d", d=64)
        A3 = AA[:].rearrange("p (h g) -> p h g", g=16)
        for h in range(16):
            def amul_fn(v, h=h):
                v.tensor_tensor(P3, K3, Q3[:, h:h + 1, :].broadcast_to([128, 16, 64]),
                                ALU.mult)
                return v.tensor_reduce(A3[:, h:h + 1, :].transpose([0, 2, 1])
                                       if False else
                                       AA[:, h * 16:(h + 1) * 16]
                                       .rearrange("p (g o) -> p g o", o=1),
                                       P3, AX, ALU.add)
            r.on("vector", amul_fn, inc="dve")
        r.on("vector", lambda v: v.tensor_reduce(
            DN[:, 0:16].rearrange("p (h o) -> p h o", o=1),
            AA[:].rearrange("p (h g) -> p h g", g=16), AX, ALU.add), inc="dve")
        r.on("vector", lambda v: v.reciprocal(DN[:, 16:32], DN[:, 0:16]), inc="dve")
        for g in range(16):
            def oacc_fn(v, g=g):
                src = AA[:].rearrange("p (h g) -> p h g", g=16)[:, :, g:g + 1] \
                    .broadcast_to([128, 16, 64])
                vv = V3[:, g:g + 1, :].broadcast_to([128, 16, 64])
                o3 = OACC[:].rearrange("p (h d) -> p h d", d=64)
                if g == 0:
                    return v.tensor_tensor(o3, src, vv, ALU.mult)
                v.tensor_tensor(P3, src, vv, ALU.mult)
                return v.tensor_tensor(OACC[:], OACC[:], PRD[:], ALU.add)
            r.on("vector", oacc_fn, inc="dve")
        r.on("vector", lambda v: v.tensor_tensor(
            ATok[:].rearrange("p (h d) -> p h d", d=64),
            OACC[:].rearrange("p (h d) -> p h d", d=64),
            DN[:, 16:32].rearrange("p (h o) -> p h o", o=1)
            .broadcast_to([128, 16, 64]), ALU.mult), inc="dve")

        for c8 in range(8):
            r.on("tensor", lambda t, c8=c8: t.transpose(
                ps_tr[:], ATok[:, c8 * 128:(c8 + 1) * 128], IDB[:]), inc="pe")
            r.on("scalar", lambda s, c8=c8: s.copy(
                AFM[:, c8 * 128:(c8 + 1) * 128], ps_tr[:]), inc="act")
        for f in range(8):
            r.dma("gpsimd", OW[:].rearrange("p (c w) -> p c w", w=128),
                  wo_i[:, f * 128:(f + 1) * 128].rearrange("(c p) w -> p c w", p=128))

            def opj_fn(t, f=f):
                for c in range(8):
                    li = t.matmul(ps_a[0:128, 0:128],
                                  OW[:, c * 128:(c + 1) * 128],
                                  AFM[:, c * 128:(c + 1) * 128],
                                  start=(c == 0), stop=(c == 7))
                return li
            r.on("tensor", opj_fn, inc="pe")
            r.on("vector", lambda v, f=f, tb=tb: v.scalar_tensor_tensor(
                X[:, f * T + tb:f * T + tb + 128], ps_a[0:128, 0:128],
                OBP[:, f:f + 1], X[:, f * T + tb:f * T + tb + 128],
                ALU.add, ALU.add), inc="dve")

    for tt_ in range(4):
        tb = tt_ * 512

        def sl_in(c, j0, w, tb=tb):
            return X[:, c * T + tb + j0: c * T + tb + j0 + w]

        def sl_out(c, j0, w):
            return ZT[:, c * 512 + j0: c * 512 + j0 + w]
        rec_ln(r, kit, tiles, sl_in, sl_out, 512, gb=(LN2G, LN2B))

        for oc in range(32):
            r.dma("gpsimd", WAR[:, 0:8 * 128].rearrange("p (c w) -> p c w", w=128),
                  fc_i[:, oc * 128:(oc + 1) * 128].rearrange("(c p) w -> p c w", p=128))

            def fc_fn(t):
                for c in range(8):
                    li = t.matmul(ps_a[:], WAR[:, c * 128:(c + 1) * 128],
                                  ZT[:, c * 512:(c + 1) * 512],
                                  start=(c == 0), stop=(c == 7))
                return li
            r.on("tensor", fc_fn, inc="pe")
            r.on("scalar", lambda s, oc=oc: s.activation(
                GT[:, oc * 512:(oc + 1) * 512], ps_a[:], AF.Gelu,
                bias=FCB[:, oc:oc + 1]), inc="act")

        for f in range(8):
            r.dma("gpsimd", WAR[:, 0:32 * 128].rearrange("p (c w) -> p c w", w=128),
                  pj_i[:, f * 128:(f + 1) * 128].rearrange("(c p) w -> p c w", p=128))

            def pj_fn(t):
                for c in range(32):
                    li = t.matmul(ps_a[:], WAR[:, c * 128:(c + 1) * 128],
                                  GT[:, c * 512:(c + 1) * 512],
                                  start=(c == 0), stop=(c == 31))
                return li
            r.on("tensor", pj_fn, inc="pe")
            r.on("vector", lambda v, f=f, tb=tb: v.scalar_tensor_tensor(
                X[:, f * T + tb:f * T + tb + 512], ps_a[:], PJB[:, f:f + 1],
                X[:, f * T + tb:f * T + tb + 512], ALU.add, ALU.add), inc="dve")

    r.on("gpsimd", lambda g: g.dma_start(
        out=X[:].rearrange("p (c w) -> p c w", w=T),
        in_=x_i[:].rearrange("(c p) w -> p c w", p=128),
        accum_op=ALU.add), inc="wdma", inc_by=16)
    r.dma("sync", xo[:].rearrange("(c p) w -> p c w", p=128),
          X[:].rearrange("p (c w) -> p c w", w=T), inc="st")
    r.on("sync", lambda e: e.nop(), waits=r.snap())
    if not ext:
        sc.close()
    if ext:
        with nc.Block() as block:
            r.finalize(block)
    return nc


def l1attn_inputs(x_fm, c, W):
    m = {
        "x": _slice_own(x_fm, c),
        "ln1g": _pp(W["L1_attn_ln1_w"]),
        "ln1b": _pp(W["L1_attn_ln1_b"]),
        "ln2g": _pp(W["L1_attn_ln2_w"]),
        "ln2b": _pp(W["L1_attn_ln2_b"]),
        "wq": W["L1_attn_q_w_t"],
        "wk": W["L1_attn_k_w_t"],
        "wv": W["L1_attn_v_w_t"],
        "wo": W["L1_attn_o_w_t"],
        "qb": np.asarray(W["L1_attn_q_b"], np.float32).reshape(1, D),
        "kb": np.asarray(W["L1_attn_k_b"], np.float32).reshape(1, D),
        "vb": np.asarray(W["L1_attn_v_b"], np.float32).reshape(1, D),
        "ob": _pp(W["L1_attn_o_b"]),
        "fc": W["L1_attn_fc_w_t"],
        "fcb": _pp(W["L1_attn_fc_b"], 32),
        "pj": W["L1_attn_proj_w_t"],
        "pjb": _pp(W["L1_attn_proj_b"]),
        "c_ident": consts()["ident"],
    }
    m.update(const_in_map(["ones128", "ones1x128"]))
    return m
